# version 01 (final)
# baseline (speedup 1.0000x reference)
import numpy as np
from contextlib import ExitStack
import concourse.bass as bass
import concourse.mybir as mybir

F32 = mybir.dt.float32
BF16 = mybir.dt.bfloat16
I32 = mybir.dt.int32
AF = mybir.ActivationFunctionType
OP = mybir.AluOpType
AX = mybir.AxisListType

TOK = 1024
NORMS = 2
D = 2048
NKT = D // 128
SEQ = 4096
LN_EPS = 1e-5
SCALE = 1.0 / float(np.sqrt(128.0))
NEG = -1e30

ENGS = ("sync", "gpsimd", "tensor", "scalar", "vector")

class Prog:
    def __init__(self, nc, block, sems):
        self.nc = nc
        self.block = block
        self.sems = sems
        self.cnt = {e: 0 for e in ENGS}
        self.waited = {e: {d: 0 for d in ENGS} for e in ENGS}

    def snap(self):
        return dict(self.cnt)

    def step(self, eng, emit, waits=None, inc=1):
        waits = waits or {}
        for d, c in waits.items():
            assert c <= self.cnt[d], (eng, d, c, self.cnt[d])
        sem_h = self.sems[eng]
        my = self.cnt

        def body(engine):
            for d, c in waits.items():
                if d == eng:
                    continue
                if c > self.waited[eng][d]:
                    engine.wait_ge(self.sems[d], c)
                    self.waited[eng][d] = c
            last = emit(engine)
            if inc and last is not None:
                last.then_inc(sem_h, inc)
        getattr(self.block, eng)(body)
        if inc:
            self.cnt[eng] += inc
        return dict(self.cnt)

    def self_wait(self, eng, count):
        def body(engine):
            engine.wait_ge(self.sems[eng], count)
        getattr(self.block, eng)(body)


def qk_scrambled_weight(w_sem, w_geo):
    g = np.asarray(w_geo).reshape(16, 64, D)
    x1 = g[:, 0:32, :].reshape(512, D)
    x2 = g[:, 32:64, :].reshape(512, D)
    blocks = [np.asarray(w_sem)]
    for t in range(4):
        blocks.append(x1[t * 128:(t + 1) * 128])
        blocks.append(x2[t * 128:(t + 1) * 128])
    return np.concatenate(blocks, axis=0)

def headmask_host():
    import ml_dtypes
    hm = np.zeros((128, 16, 16), np.float32)
    for k in range(16):
        if k < 8:
            hm[0:64, k, 2 * k] = 1.0
            hm[64:128, k, 2 * k + 1] = 1.0
        else:
            t = (k - 8) // 2
            for j in range(4):
                hm[32 * j:32 * j + 32, k, 4 * t + j] = 1.0
    return np.ascontiguousarray(hm.reshape(128, 256).astype(ml_dtypes.bfloat16))

def rope_tile(positions, kind):
    inv = 1.0 / (10000.0 ** (np.arange(0, 64, 2, dtype=np.float64) / 64.0))
    ang = positions[None, :].astype(np.float64) * inv[:, None]
    t = np.cos(ang) if kind == "cos" else np.sin(ang)
    return np.ascontiguousarray(np.tile(t, (4, 1)).astype(np.float32))


def build_A():
    nc = bass.Bass("TRN2", target_bir_lowering=False)
    xT = nc.dram_tensor("xT", [D, TOK], F32, kind="ExternalInput").ap()
    wqkvT = nc.dram_tensor("wqkvT", [D, 3 * D], F32, kind="ExternalInput").ap()
    ropeC = nc.dram_tensor("ropeC", [128, TOK], F32, kind="ExternalInput").ap()
    ropeS = nc.dram_tensor("ropeS", [128, TOK], F32, kind="ExternalInput").ap()
    ffnwT = nc.dram_tensor("ffnwT", [D, D], F32, kind="ExternalInput").ap()
    ffnvec = nc.dram_tensor("ffnvec", [128, 64], F32, kind="ExternalInput").ap()
    headmask = nc.dram_tensor("headmask", [128, 256], BF16, kind="ExternalInput").ap()
    accT = nc.dram_tensor("accT", [D, TOK], F32, kind="ExternalOutput").ap()
    qT = nc.dram_tensor("qT", [D, TOK], BF16, kind="ExternalOutput").ap()
    kT = nc.dram_tensor("kT", [D, TOK], BF16, kind="ExternalOutput").ap()
    vT = nc.dram_tensor("vT", [D, TOK], BF16, kind="ExternalOutput").ap()
    qkn = nc.dram_tensor("qkn", [16, 2], F32, kind="ExternalOutput").ap()
    qkv_out = {"q": qT, "k": kT, "v": vT}

    with ExitStack() as ctx:
        arA = ctx.enter_context(nc.sbuf_tensor("arA", [128, NKT * TOK], F32))
        arB = ctx.enter_context(nc.sbuf_tensor("arB", [128, NKT * TOK], F32))
        arC = ctx.enter_context(nc.sbuf_tensor("arC", [128, NKT * TOK], BF16))
        arD = ctx.enter_context(nc.sbuf_tensor("arD", [128, NKT * TOK], BF16))
        scr = ctx.enter_context(nc.sbuf_tensor("scr", [128, 1024], F32))
        statb = ctx.enter_context(nc.sbuf_tensor("statb", [128, 2048], F32))
        stat = statb[0:1, :]
        ones1 = ctx.enter_context(nc.sbuf_tensor("ones1", [128, 1], F32))
        onesr = ctx.enter_context(nc.sbuf_tensor("onesr", [1, 128], F32))
        vecs = ctx.enter_context(nc.sbuf_tensor("vecs", [128, 80], F32))
        hmask = ctx.enter_context(nc.sbuf_tensor("hmask", [128, 256], BF16))
        qkns = ctx.enter_context(nc.sbuf_tensor("qkns", [16, 2], F32))
        psX = ctx.enter_context(nc.psum_tensor("psX", [16, 2048], F32))
        psS = psX[0:1, 0:1024]
        psQ = psX[0:1, 1024:2048]
        psM = ctx.enter_context(nc.psum_tensor("psM", [128, 2048], F32))
        psB = psM[:, 1024:2048]
        s_sync = ctx.enter_context(nc.semaphore("s_sync"))
        s_gp = ctx.enter_context(nc.semaphore("s_gp"))
        s_pe = ctx.enter_context(nc.semaphore("s_pe"))
        s_ac = ctx.enter_context(nc.semaphore("s_ac"))
        s_ve = ctx.enter_context(nc.semaphore("s_ve"))
        block = ctx.enter_context(nc.Block())
        P = Prog(nc, block, dict(sync=s_sync, gpsimd=s_gp, tensor=s_pe, scalar=s_ac, vector=s_ve))

        def A_t(k): return arA[:, k * TOK:(k + 1) * TOK]
        def B_t(k): return arB[:, k * TOK:(k + 1) * TOK]
        def C_t(k): return arC[:, k * TOK:(k + 1) * TOK]
        def D_t(k): return arD[:, k * TOK:(k + 1) * TOK]
        arB_bf = arB[:].bitcast(BF16)
        arD_f32 = arD[:].bitcast(F32)
        psM_s = [psM[:, 0:1024], psM[:, 1024:2048]]

        for k in range(NKT):
            P.step("sync", lambda e, k=k: e.dma_start(out=A_t(k), in_=xT[k * 128:(k + 1) * 128, :]), inc=16)
        LD_X = P.snap()
        P.step("sync", lambda e: e.dma_start(out=vecs[:, 0:64], in_=ffnvec[:, :]), inc=16)
        P.step("sync", lambda e: e.dma_start(out=hmask[:], in_=headmask[:]), inc=16)
        LD_VECS = P.snap()
        def ones_emit(e):
            e.memset(ones1[:], 1.0)
            return e.memset(onesr[:], 1.0)
        P.step("vector", ones_emit)
        VE_ONES = P.snap()

        def emit_ln(src_t, ready):
            pe0 = P.cnt["tensor"]
            for k in range(NKT):
                w = dict(ready)
                if k >= 1:
                    w["tensor"] = pe0 + k
                sq = P.step("scalar",
                    lambda e, k=k: e.activation(scr[:, 0:TOK], src_t(k), AF.Square),
                    waits=w)
                def mm_k(e, k=k):
                    for c in range(2):
                        e.matmul(psS[0:1, c * 512:(c + 1) * 512], ones1[:],
                                 src_t(k)[:, c * 512:(c + 1) * 512],
                                 start=(k == 0), stop=(k == NKT - 1))
                    last = None
                    for c in range(2):
                        last = e.matmul(psQ[0:1, c * 512:(c + 1) * 512], ones1[:],
                                        scr[:, c * 512:(c + 1) * 512],
                                        start=(k == 0), stop=(k == NKT - 1))
                    return last
                w2 = dict(ready); w2.update({"scalar": sq["scalar"], "vector": VE_ONES["vector"]})
                P.step("tensor", mm_k, waits=w2)
            PE_STATS = P.snap()

            def var_emit(e):
                e.tensor_scalar(stat[0:1, 0:TOK], psS[0:1, :], 1.0 / D, None, OP.mult)
                e.tensor_scalar(stat[0:1, TOK:2 * TOK], psQ[0:1, :], 1.0 / D, None, OP.mult)
                e.tensor_tensor(scr[0:1, 0:TOK], stat[0:1, 0:TOK], stat[0:1, 0:TOK], OP.mult)
                e.tensor_tensor(stat[0:1, TOK:2 * TOK], stat[0:1, TOK:2 * TOK], scr[0:1, 0:TOK], OP.subtract)
                return e.tensor_scalar(stat[0:1, TOK:2 * TOK], stat[0:1, TOK:2 * TOK], LN_EPS, None, OP.add)
            V_VAR = P.step("vector", var_emit, waits={"tensor": PE_STATS["tensor"]})

            def rstd_emit(e):
                e.activation(stat[0:1, TOK:2 * TOK], stat[0:1, TOK:2 * TOK], AF.Ln)
                return e.activation(stat[0:1, TOK:2 * TOK], stat[0:1, TOK:2 * TOK], AF.Exp, scale=-0.5)
            A_RSTD = P.step("scalar", rstd_emit, waits={"vector": V_VAR["vector"]})

            def bc(e, src_off):
                last = None
                for c in range(2):
                    last = e.matmul(psB[:, c * 512:(c + 1) * 512], onesr[:],
                                    stat[0:1, src_off + c * 512:src_off + (c + 1) * 512],
                                    start=True, stop=True)
                return last
            M1 = P.step("tensor", lambda e: bc(e, 0), waits={"scalar": A_RSTD["scalar"], "vector": V_VAR["vector"]})
            E1 = P.step("vector", lambda e: e.tensor_copy(statb[:, 0:TOK], psB[:]),
                        waits={"tensor": M1["tensor"]})
            M2 = P.step("tensor", lambda e: bc(e, TOK), waits={"vector": E1["vector"]})
            E2 = P.step("vector", lambda e: e.tensor_copy(statb[:, TOK:2 * TOK], psB[:]),
                        waits={"tensor": M2["tensor"]})
            P.self_wait("vector", E2["vector"])
            return E2

        LN1 = emit_ln(A_t, {"sync": LD_X["sync"]})
        def xn_emit(e):
            last = None
            for k in range(NKT):
                e.tensor_tensor(B_t(k), A_t(k), statb[:, 0:TOK], OP.subtract)
                e.tensor_tensor(B_t(k), B_t(k), statb[:, TOK:2 * TOK], OP.mult)
                last = e.tensor_copy(C_t(k), B_t(k))
            return last
        XN = P.step("vector", xn_emit, waits={"tensor": LN1["tensor"] if False else P.cnt["tensor"]})

        LN2 = emit_ln(B_t, {"vector": XN["vector"]})

        ve0 = P.cnt["vector"]
        ac0 = P.cnt["scalar"]
        for k in range(NKT):
            sl = psM[:, (k % 2) * 1024:(k % 2) * 1024 + 1024]
            w = {"sync": LD_VECS["sync"]}
            if k >= 2:
                w["scalar"] = ac0 + k - 1
            def a_v(e, k=k, sl=sl):
                e.tensor_tensor(sl, B_t(k), statb[:, 0:TOK], OP.subtract)
                e.tensor_tensor(sl, sl, statb[:, TOK:2 * TOK], OP.mult)
                return e.tensor_scalar(sl, sl, vecs[:, k:k + 1], vecs[:, 16 + k:17 + k], OP.mult, OP.add)
            vm = P.step("vector", a_v, waits=w)
            P.step("scalar", lambda e, k=k, sl=sl: e.activation(D_t(k), sl, AF.Sign),
                   waits={"vector": vm["vector"]})
        A_DONE = P.snap()

        wslot = [arB[:, 0:2048], arB[:, 2048:4096]]
        ld0 = P.cnt["sync"]
        pe0 = P.cnt["tensor"]
        for k in range(NKT):
            w = {"vector": A_DONE["vector"], "scalar": A_DONE["scalar"]}
            if k >= 2:
                w["tensor"] = pe0 + k - 1
            P.step("sync", lambda e, k=k: e.dma_start(out=wslot[k % 2],
                   in_=ffnwT[k * 128:(k + 1) * 128, :]), waits=w, inc=16)
            def cs(e, k=k):
                last = None
                for c in range(4):
                    tgt = psS if c < 2 else psQ
                    last = e.matmul(tgt[0:1, (c % 2) * 512:(c % 2 + 1) * 512], ones1[:],
                                    wslot[k % 2][:, c * 512:(c + 1) * 512],
                                    start=(k == 0), stop=(k == NKT - 1))
                return last
            P.step("tensor", cs, waits={"sync": ld0 + 16 * (k + 1)})
        PE_CM = P.snap()
        def cm_emit(e):
            e.tensor_scalar(stat[0:1, 0:1024], psS[0:1, :], 1.0 / D, None, OP.mult)
            return e.tensor_scalar(stat[0:1, 1024:2048], psQ[0:1, :], 1.0 / D, None, OP.mult)
        V_CM = P.step("vector", cm_emit, waits={"tensor": PE_CM["tensor"]})
        def bc2(e, half):
            last = None
            for c in range(2):
                last = e.matmul(psB[:, c * 512:(c + 1) * 512], onesr[:],
                                stat[0:1, half * 1024 + c * 512:half * 1024 + (c + 1) * 512],
                                start=True, stop=True)
            return last
        B1 = P.step("tensor", lambda e: bc2(e, 0), waits={"vector": V_CM["vector"]})
        Bv1 = P.step("vector", lambda e: e.tensor_copy(statb[:, 0:1024], psB[:]), waits={"tensor": B1["tensor"]})
        B2 = P.step("tensor", lambda e: bc2(e, 1), waits={"vector": Bv1["vector"]})
        Bv2 = P.step("vector", lambda e: e.tensor_copy(statb[:, 1024:2048], psB[:]), waits={"tensor": B2["tensor"]})

        BA = P.step("vector", lambda e: e.tensor_tensor(vecs[:, 64:80], vecs[:, 32:48], vecs[:, 48:64], OP.mult),
                    waits={"sync": LD_VECS["sync"]})

        def swt_tile(kk):
            return arB_bf[:, 16384 + kk * 2048:16384 + (kk + 1) * 2048]
        EVAC = {}
        for half in range(2):
            ld_marks = []
            ve_marks = []
            ac_marks = []
            ldh = P.cnt["sync"]
            for i in range(8):
                k = half * 8 + i
                w = {"vector": Bv2["vector"]}
                if i >= 2:
                    w["scalar"] = ac_marks[i - 2]
                if half == 1 and i < 2:
                    w["scalar"] = ac_marks_prev[i + 6]
                P.step("sync", lambda e, k=k, i=i: e.dma_start(out=wslot[i % 2],
                       in_=ffnwT[k * 128:(k + 1) * 128, :]), waits=w, inc=16)
                ld_marks.append(P.cnt["sync"])
                vm = P.step("vector", lambda e, i=i: e.tensor_tensor(wslot[i % 2], wslot[i % 2], statb[:], OP.subtract),
                            waits={"sync": ld_marks[i]})
                ve_marks.append(vm["vector"])
                am = P.step("scalar", lambda e, i=i: e.activation(swt_tile(i), wslot[i % 2], AF.Sign),
                            waits={"vector": vm["vector"]})
                ac_marks.append(am["scalar"])
                if i + 2 < 8:
                    pass
            ac_marks_prev = ac_marks
            SW = P.snap()
            for o in range(NKT):
                slot = psM_s[o % 2]
                def ffn_mm(e, o=o, slot=slot):
                    last = None
                    for i in range(8):
                        for c in range(2):
                            last = e.matmul(slot[:, c * 512:(c + 1) * 512],
                                            swt_tile(i)[:, o * 128:(o + 1) * 128],
                                            arD[:, (half * 8 + i) * TOK + c * 512:(half * 8 + i) * TOK + (c + 1) * 512],
                                            start=(i == 0), stop=(i == 7))
                    return last
                w = {"scalar": ac_marks[-1], "vector": BA["vector"]}
                if o >= 2:
                    w["vector"] = max(w["vector"], EVAC[(half, o - 2)])
                MM = P.step("tensor", ffn_mm, waits=w)
                def ev(e, o=o, slot=slot, half=half):
                    last = e.scalar_tensor_tensor(A_t(o), slot, vecs[:, 48 + o:49 + o], A_t(o), OP.mult, OP.add)
                    if half == 0:
                        last = e.tensor_scalar(A_t(o), A_t(o), vecs[:, 64 + o:65 + o], None, OP.add)
                    return last
                EV = P.step("vector", ev, waits={"tensor": MM["tensor"]})
                EVAC[(half, o)] = EV["vector"]
        FFN_DONE = P.snap()

        for o in range(NKT):
            P.step("gpsimd", lambda e, o=o: e.dma_start(out=accT[o * 128:(o + 1) * 128, :], in_=A_t(o)),
                   waits={"vector": EVAC[(1, o)]}, inc=16)
        ST_ACC = P.snap()

        RT = P.step("sync", lambda e: e.dma_start(out=arD_f32[:, 0:1024], in_=ropeC[:]),
                    waits={"tensor": FFN_DONE["tensor"]}, inc=16)
        RT = P.step("sync", lambda e: e.dma_start(out=arD_f32[:, 1024:2048], in_=ropeS[:]), inc=16)
        COS = arD_f32[:, 0:1024]
        SIN = arD_f32[:, 1024:2048]
        def stage(i):
            return arD[:, 8192 + i * 1024:8192 + (i + 1) * 1024]

        qk_slot = [arB_bf[:, 0:8192], arB_bf[:, 8192:16384]]
        gp0 = P.cnt["gpsimd"]
        pe_marks = {}
        pend_pair = None
        scrN = arA[:, 0:512].bitcast(BF16)
        psN = psX[0:16, :]
        normacc = arA[0:16, 512:2560]
        NIN = P.step("vector", lambda e: e.memset(normacc, 0.0), waits={"gpsimd": ST_ACC["gpsimd"]})
        for cch in range(12):
            w = {"vector": FFN_DONE["vector"], "scalar": FFN_DONE["scalar"]}
            if cch >= 2:
                w["tensor"] = pe_marks[cch - 2]
            P.step("gpsimd", lambda e, cch=cch: e.dma_start(
                out=qk_slot[cch % 2].rearrange("p (k n) -> p k n", k=16),
                in_=wqkvT[:, cch * 512:(cch + 1) * 512].rearrange("(k p) n -> p k n", p=128)),
                waits=w, inc=16)
            GLD = P.snap()
            for ot in range(4):
                g = cch * 4 + ot
                slot = psM_s[g % 2]
                def qkv_mm(e, cch=cch, ot=ot, slot=slot):
                    last = None
                    for tc in range(2):
                        for k in range(16):
                            last = e.matmul(slot[:, tc * 512:(tc + 1) * 512],
                                            qk_slot[cch % 2][:, k * 512 + ot * 128:k * 512 + (ot + 1) * 128],
                                            arC[:, k * TOK + tc * 512:k * TOK + (tc + 1) * 512],
                                            start=(k == 0), stop=(k == 15))
                    return last
                w2 = {"gpsimd": GLD["gpsimd"]}
                if g >= 2:
                    if ("evac", g - 2) in pe_marks:
                        w2["scalar"] = pe_marks[("evac", g - 2)]
                    if ("evacN", g - 2) in pe_marks:
                        w2["scalar"] = max(w2.get("scalar", 0), pe_marks[("evacN", g - 2)])
                    if ("evacv", g - 2) in pe_marks:
                        w2["vector"] = pe_marks[("evacv", g - 2)]
                MM = P.step("tensor", qkv_mm, waits=w2)
                pe_marks[cch] = MM["tensor"]
                if g < 32 and NORMS >= 1:
                    ti = g % 16
                    wsq = {"tensor": MM["tensor"], "gpsimd": ST_ACC["gpsimd"]}
                    if ("normmm", g - 1) in pe_marks:
                        wsq["tensor"] = max(wsq["tensor"], pe_marks[("normmm", g - 1)])
                    SQ = P.step("scalar", lambda e, slot=slot: e.activation(scrN, slot, AF.Square), waits=wsq)
                    pe_marks[("evacN", g)] = SQ["scalar"]
                    if NORMS < 2:
                        continue
                    off = 0 if g < 16 else 1024
                    def nmm(e, g=g, off=off):
                        last = None
                        for c2 in range(2):
                            last = e.matmul(psN[:, off + c2 * 512:off + (c2 + 1) * 512],
                                            hmask[:, 16 * (g % 16):16 * (g % 16) + 16], scrN[:, c2 * 512:(c2 + 1) * 512],
                                            start=True, stop=True)
                        return last
                    wmm = {"scalar": SQ["scalar"], "sync": LD_VECS["sync"] + 16}
                    if ("nadd", g - 1) in pe_marks:
                        wmm["vector"] = pe_marks[("nadd", g - 1)]
                    NM = P.step("tensor", nmm, waits=wmm)
                    pe_marks[("normmm", g)] = NM["tensor"]
                    NA = P.step("vector", lambda e, off=off: e.tensor_tensor(
                        normacc[:, off:off + 1024], normacc[:, off:off + 1024], psN[:, off:off + 1024], OP.add),
                        waits={"tensor": NM["tensor"], "vector": NIN["vector"]})
                    pe_marks[("nadd", g)] = NA["vector"]
                is_geo = (8 <= g < 16) or (24 <= g < 32)
                if not is_geo:
                    sg = g % 4
                    wst = {"tensor": MM["tensor"]}
                    if ("store", g - 4) in pe_marks:
                        wst["gpsimd"] = pe_marks[("store", g - 4)]
                    AC = P.step("scalar", lambda e, g=g, slot=slot, sg=sg: e.activation(stage(sg), slot, AF.Copy),
                                waits=wst)
                    pe_marks[("evac", g)] = AC["scalar"]
                    if g >= 32:
                        tgt, row = qkv_out["v"], (g - 32) * 128
                        STm = P.step("gpsimd", lambda e, tgt=tgt, row=row, sg=sg: e.dma_start(
                            out=tgt[row:row + 128, :], in_=stage(sg)), waits={"scalar": AC["scalar"]}, inc=16)
                    else:
                        tgt = qkv_out["q"] if g < 16 else qkv_out["k"]
                        t2 = g % 16
                        P.step("gpsimd", lambda e, tgt=tgt, t2=t2, sg=sg: e.dma_start(
                            out=tgt[128 * (2 * t2):128 * (2 * t2) + 64, :], in_=stage(sg)[0:64, :]),
                            waits={"scalar": AC["scalar"]}, inc=16)
                        STm = P.step("gpsimd", lambda e, tgt=tgt, t2=t2, sg=sg: e.dma_start(
                            out=tgt[128 * (2 * t2 + 1):128 * (2 * t2 + 1) + 64, :], in_=stage(sg)[64:128, :]), inc=16)
                    pe_marks[("store", g)] = STm["gpsimd"]
                else:
                    if g % 2 == 0:
                        pend_pair = (g, slot, MM)
                    else:
                        g1, slot1, MM1 = pend_pair
                        def rope_ev(e, slot1=slot1, slot2=slot, g1=g1):
                            sg1, sg2 = g1 % 4, (g1 + 1) % 4
                            e.tensor_tensor(statb[:, 0:1024], slot1, COS, OP.mult)
                            e.tensor_tensor(scr[:, 0:1024], slot2, SIN, OP.mult)
                            e.tensor_tensor(statb[:, 0:1024], statb[:, 0:1024], scr[:, 0:1024], OP.subtract)
                            e.tensor_copy(stage(sg1), statb[:, 0:1024])
                            e.tensor_tensor(statb[:, 1024:2048], slot1, SIN, OP.mult)
                            e.tensor_tensor(scr[:, 0:1024], slot2, COS, OP.mult)
                            e.tensor_tensor(statb[:, 1024:2048], statb[:, 1024:2048], scr[:, 0:1024], OP.add)
                            return e.tensor_copy(stage(sg2), statb[:, 1024:2048])
                        wro = {"tensor": MM["tensor"], "sync": RT["sync"]}
                        if ("store", g1 - 4) in pe_marks:
                            wro["gpsimd"] = pe_marks[("store", g1 - 4)]
                        if ("store", g - 4) in pe_marks:
                            wro["gpsimd"] = max(wro.get("gpsimd", 0), pe_marks[("store", g - 4)])
                        VV = P.step("vector", rope_ev, waits=wro)
                        pe_marks[("evacv", g1)] = VV["vector"]
                        pe_marks[("evacv", g1 + 1)] = VV["vector"]
                        tgt = qkv_out["q"] if g < 16 else qkv_out["k"]
                        u = ((g1 % 16) - 8) // 2
                        for j4 in range(4):
                            P.step("gpsimd", lambda e, tgt=tgt, u=u, g1=g1, j4=j4: e.dma_start(
                                out=tgt[128 * (4 * u + j4) + 64:128 * (4 * u + j4) + 96, :],
                                in_=stage(g1 % 4)[32 * j4:32 * j4 + 32, :]),
                                waits={"vector": VV["vector"]}, inc=16)
                            STm = P.step("gpsimd", lambda e, tgt=tgt, u=u, g1=g1, j4=j4: e.dma_start(
                                out=tgt[128 * (4 * u + j4) + 96:128 * (4 * u + j4) + 128, :],
                                in_=stage((g1 + 1) % 4)[32 * j4:32 * j4 + 32, :]), inc=16)
                        pe_marks[("store", g1)] = STm["gpsimd"]
                        pe_marks[("store", g1 + 1)] = STm["gpsimd"]
        if NORMS < 2:
            def nmax(e):
                e.memset(qkns[:], 0.0)
                return e.tensor_copy(qkns[:], qkns[:])
            NMX = P.step("vector", nmax)
            P.self_wait("vector", NMX["vector"])
            P.step("gpsimd", lambda e: e.dma_start(out=qkn[:], in_=qkns[:]), waits={"vector": NMX["vector"]}, inc=16)
            P.self_wait("gpsimd", P.cnt["gpsimd"])
            return nc
        def nmax(e):
            e.reduce_max(qkns[:, 0:1], normacc[:, 0:1024], axis=AX.X)
            return e.reduce_max(qkns[:, 1:2], normacc[:, 1024:2048], axis=AX.X)
        NMX = P.step("vector", nmax, waits={"vector": pe_marks[("nadd", 31)]})
        P.self_wait("vector", NMX["vector"])
        P.step("gpsimd", lambda e: e.dma_start(out=qkn[:], in_=qkns[:]), waits={"vector": NMX["vector"]}, inc=16)
        P.self_wait("gpsimd", P.cnt["gpsimd"])
    return nc



def build_B():
    nc = bass.Bass("TRN2", target_bir_lowering=False)
    accT = nc.dram_tensor("accT", [D, TOK], F32, kind="ExternalInput").ap()
    qTo = nc.dram_tensor("qTo", [D, TOK], BF16, kind="ExternalInput").ap()
    kTo = nc.dram_tensor("kTo", [D, TOK], BF16, kind="ExternalInput").ap()
    kfull = nc.dram_tensor("kfull", [D, SEQ], BF16, kind="ExternalInput").ap()
    vhost = nc.dram_tensor("vhost", [16 * 128, SEQ], BF16, kind="ExternalInput").ap()
    vown = nc.dram_tensor("vown", [16 * 128, TOK], BF16, kind="ExternalInput").ap()
    maskcol = nc.dram_tensor("maskcol", [128, 64], F32, kind="ExternalInput").ap()
    trimask = nc.dram_tensor("trimask", [128, 2048], BF16, kind="ExternalInput").ap()
    qk2 = nc.dram_tensor("qk2", [16, 8], F32, kind="ExternalInput").ap()
    woT = nc.dram_tensor("woT", [D, D], F32, kind="ExternalInput").ap()
    winT = nc.dram_tensor("winT", [D, 4384], F32, kind="ExternalInput").ap()
    x1T = nc.dram_tensor("x1T", [D, TOK], F32, kind="ExternalOutput").ap()
    xn1bT = nc.dram_tensor("xn1bT", [D, TOK], BF16, kind="ExternalOutput").ap()
    gate = nc.dram_tensor("gate", [TOK, D], BF16, kind="ExternalOutput").ap()
    hbcdt = nc.dram_tensor("hbcdt", [TOK, 2336], BF16, kind="ExternalOutput").ap()

    with ExitStack() as ctx:
        arAcc = ctx.enter_context(nc.sbuf_tensor("arAcc", [128, NKT * TOK], F32))
        arStr = ctx.enter_context(nc.sbuf_tensor("arStr", [128, 2 * 11264], BF16))
        arP = ctx.enter_context(nc.sbuf_tensor("arP", [128, 36 * 512], BF16))
        arO = ctx.enter_context(nc.sbuf_tensor("arO", [128, NKT * TOK], BF16))
        lac = ctx.enter_context(nc.sbuf_tensor("lac", [128, 512], F32))
        bias = ctx.enter_context(nc.sbuf_tensor("bias", [128, 1024 + 64], F32))
        trim = ctx.enter_context(nc.sbuf_tensor("trim", [128, 2048], BF16))
        mcol = ctx.enter_context(nc.sbuf_tensor("mcol", [128, 64], F32))
        q2s = ctx.enter_context(nc.sbuf_tensor("q2s", [16, 8], F32))
        stat2 = ctx.enter_context(nc.sbuf_tensor("stat2", [128, 2048], F32))
        ones1 = ctx.enter_context(nc.sbuf_tensor("ones1", [128, 1], F32))
        onesr = ctx.enter_context(nc.sbuf_tensor("onesr", [1, 128], F32))
        scr = ctx.enter_context(nc.sbuf_tensor("scr", [128, 1024], F32))
        psA = ctx.enter_context(nc.psum_tensor("psA", [128, 2048], F32))
        psM = ctx.enter_context(nc.psum_tensor("psM", [128, 2048], F32))
        s_sync = ctx.enter_context(nc.semaphore("s_sync"))
        s_gp = ctx.enter_context(nc.semaphore("s_gp"))
        s_pe = ctx.enter_context(nc.semaphore("s_pe"))
        s_ac = ctx.enter_context(nc.semaphore("s_ac"))
        s_ve = ctx.enter_context(nc.semaphore("s_ve"))
        block = ctx.enter_context(nc.Block())

        P = Prog(nc, block, dict(sync=s_sync, gpsimd=s_gp, tensor=s_pe, scalar=s_ac, vector=s_ve))
        psS2 = [psA[:, 0:512], psA[:, 512:1024]]
        psO = psA[:, 1024:1536]
        psBC = psA[:, 1536:2048]
        psL = psM[0:1, 0:512]
        psM_s = [psM[:, 0:1024], psM[:, 1024:2048]]

        def Acc_t(k): return arAcc[:, k * TOK:(k + 1) * TOK]
        def P_t(i): return arP[:, i * 512:(i + 1) * 512]
        def O_t(h): return arO[:, h * TOK:(h + 1) * TOK]

        def st_q(b): return arStr[:, b * 11264 + 0:b * 11264 + 1024]
        def st_kow(b): return arStr[:, b * 11264 + 1024:b * 11264 + 2048]
        def st_vow(b): return arStr[:, b * 11264 + 2048:b * 11264 + 3072]
        def st_khd(b): return arStr[:, b * 11264 + 3072:b * 11264 + 7168]
        def st_vhd(b): return arStr[:, b * 11264 + 7168:b * 11264 + 11264]

        for k in range(NKT):
            P.step("sync", lambda e, k=k: e.dma_start(out=Acc_t(k), in_=accT[k * 128:(k + 1) * 128, :]), inc=16)
        P.step("sync", lambda e: e.dma_start(out=trim[:], in_=trimask[:]), inc=16)
        P.step("sync", lambda e: e.dma_start(out=mcol[:], in_=maskcol[:]), inc=16)
        P.step("sync", lambda e: e.dma_start(out=q2s[:], in_=qk2[:]), inc=16)
        LD0 = P.snap()
        P.step("vector", lambda e: (e.memset(ones1[:], 1.0), e.memset(onesr[:], 1.0))[-1])
        VE_ONES = P.snap()

        def m_prep_v(e):
            e.reduce_max(q2s[:, 5:6], q2s[:, 1:5], axis=AX.X)
            return e.tensor_copy(q2s[:, 6:7], q2s[:, 5:6])
        MV = P.step("vector", m_prep_v, waits={"sync": LD0["sync"]})
        P.self_wait("vector", MV["vector"])
        MV2 = P.step("vector", lambda e: e.tensor_tensor(q2s[:, 6:7], q2s[:, 0:1], q2s[:, 5:6], OP.mult))
        MS = P.step("scalar", lambda e: e.activation(q2s[:, 7:8], q2s[:, 6:7], AF.Sqrt),
                    waits={"vector": MV2["vector"]})
        MN = P.step("vector", lambda e: e.tensor_scalar(q2s[:, 7:8], q2s[:, 7:8], -SCALE * 1.01, None, OP.mult),
                    waits={"scalar": MS["scalar"]})
        P.self_wait("vector", MN["vector"])
        MT = P.step("gpsimd", lambda e: e.dma_start(out=stat2[0:1, 0:16], in_=q2s[:, 7:8]),
                    waits={"vector": MN["vector"]}, inc=16)
        BC = P.step("tensor", lambda e: e.matmul(psBC[:, 0:16], onesr[:], stat2[0:1, 0:16], start=True, stop=True),
                    waits={"gpsimd": MT["gpsimd"], "vector": VE_ONES["vector"]})
        BCE = P.step("vector", lambda e: e.tensor_copy(bias[:, 1024:1040], psBC[:, 0:16]),
                     waits={"tensor": BC["tensor"]})
        bc_marks = []
        for h in range(16):
            bm = P.step("vector", lambda e, h=h: e.tensor_scalar(
                bias[:, h * 64:(h + 1) * 64], mcol[:], bias[:, 1024 + h:1025 + h], None, OP.add),
                waits={"vector": BCE["vector"], "sync": LD0["sync"]})
            bc_marks.append(bm["vector"])
        BPREP = P.snap()

        pe_marks = {}
        for h in range(16):
            b = h % 2
            wld = {}
            if h >= 2:
                wld = {"tensor": pe_marks[("hdone", h - 2)]}
            P.step("sync", lambda e, h=h, b=b: e.dma_start(out=st_q(b), in_=qTo[128 * h:128 * h + 128, :]),
                   waits=wld, inc=16)
            P.step("sync", lambda e, h=h, b=b: e.dma_start(out=st_kow(b), in_=kTo[128 * h:128 * h + 128, :]), inc=16)
            P.step("sync", lambda e, h=h, b=b: e.dma_start(out=st_vow(b), in_=vown[128 * h:128 * h + 128, :]), inc=16)
            P.step("sync", lambda e, h=h, b=b: e.dma_start(out=st_khd(b), in_=kfull[128 * h:128 * h + 128, :]), inc=16)
            HLD = P.step("sync", lambda e, h=h, b=b: e.dma_start(out=st_vhd(b), in_=vhost[128 * h:128 * h + 128, :]), inc=16)
            for g in range(2):
                exp_marks = {}
                ve_tri = {}
                for K in range(36):
                    sl = psS2[K % 2]
                    w = {"sync": HLD["sync"], "vector": bc_marks[h]}
                    if K >= 2:
                        w["scalar"] = exp_marks[K - 2]
                    if K < 32:
                        MMs = P.step("tensor", lambda e, K=K, sl=sl, b=b, g=g: e.matmul(
                            sl, st_khd(b)[:, 128 * K:128 * K + 128], st_q(b)[:, 512 * g:512 * g + 512],
                            start=True, stop=True), waits=w)
                        EX = P.step("scalar", lambda e, K=K, sl=sl, h=h, g=g: e.activation(
                            P_t(K), sl, AF.Exp, bias=bias[:, h * 64 + 2 * K + g:h * 64 + 2 * K + g + 1],
                            scale=SCALE), waits={"tensor": MMs["tensor"]})
                    else:
                        j = K - 32
                        MMs = P.step("tensor", lambda e, j=j, sl=sl, b=b, g=g: e.matmul(
                            sl, st_kow(b)[:, 512 * g + 128 * j:512 * g + 128 * j + 128],
                            st_q(b)[:, 512 * g:512 * g + 512], start=True, stop=True), waits=w)
                        TRI = P.step("vector", lambda e, j=j, sl=sl: e.tensor_tensor(
                            sl, sl, trim[:, 512 * j:512 * j + 512], OP.add), waits={"tensor": MMs["tensor"]})
                        EX = P.step("scalar", lambda e, K=K, sl=sl, h=h: e.activation(
                            P_t(K), sl, AF.Exp, bias=bias[:, 1024 + h:1025 + h], scale=SCALE),
                            waits={"vector": TRI["vector"]})
                    exp_marks[K] = EX["scalar"]
                def lsum_emit(e):
                    e.tensor_tensor(lac[:], P_t(0), P_t(1), OP.add)
                    last = None
                    for K in range(2, 36):
                        last = e.tensor_tensor(lac[:], lac[:], P_t(K), OP.add)
                    return last
                LS = P.step("vector", lsum_emit, waits={"scalar": exp_marks[35]})
                def pv_emit(e):
                    last = None
                    for K in range(36):
                        if K < 32:
                            lhs = st_vhd(b)[:, 128 * K:128 * K + 128]
                        else:
                            j = K - 32
                            lhs = st_vow(b)[:, 512 * g + 128 * j:512 * g + 128 * j + 128]
                        last = e.matmul(psO, lhs, P_t(K), start=(K == 0), stop=(K == 35))
                    return last
                wpv = {"scalar": exp_marks[35]}
                if ("oe_prev",) in pe_marks:
                    wpv["vector"] = pe_marks[("oe_prev",)]
                PV = P.step("tensor", pv_emit, waits=wpv)
                LM = P.step("tensor", lambda e: e.matmul(psL, ones1[:], lac[:], start=True, stop=True),
                            waits={"vector": LS["vector"]})
                LE = P.step("vector", lambda e: e.tensor_copy(stat2[0:1, 16:528], psL),
                            waits={"tensor": LM["tensor"]})
                P.self_wait("vector", LE["vector"])
                RC = P.step("vector", lambda e: e.reciprocal(stat2[0:1, 528:1040], stat2[0:1, 16:528]))
                P.self_wait("vector", RC["vector"])
                RB = P.step("tensor", lambda e: e.matmul(psBC, onesr[:], stat2[0:1, 528:1040], start=True, stop=True),
                            waits={"vector": RC["vector"]})
                RBE = P.step("vector", lambda e: e.tensor_copy(stat2[:, 1024:1536], psBC),
                             waits={"tensor": RB["tensor"]})
                OE = P.step("vector", lambda e, h=h, g=g: e.tensor_tensor(
                    O_t(h)[:, 512 * g:512 * g + 512], psO, stat2[:, 1024:1536], OP.mult),
                    waits={"tensor": PV["tensor"], "vector": RBE["vector"]})
                pe_marks[("oe", h, g)] = OE["vector"]
                pe_marks[("oe_prev",)] = OE["vector"]
            pe_marks[("hdone", h)] = P.cnt["tensor"]
        ATT = P.snap()

        wslot = [arStr[:, 0:8192], arStr[:, 11264:11264 + 8192]]
        st_marks = {}
        for cch in range(4):
            w = {"vector": pe_marks[("oe", 15, 1)]} if cch < 2 else {"tensor": st_marks[cch - 2]}
            GL = P.step("gpsimd", lambda e, cch=cch: e.dma_start(
                out=wslot[cch % 2].rearrange("p (k n) -> p k n", k=16),
                in_=woT[:, cch * 512:(cch + 1) * 512].rearrange("(k p) n -> p k n", p=128)),
                waits=w, inc=16)
            for ot in range(4):
                o = cch * 4 + ot
                slot = psM_s[o % 2]
                def wo_mm(e, cch=cch, ot=ot, slot=slot):
                    last = None
                    for tc in range(2):
                        for k in range(16):
                            last = e.matmul(slot[:, tc * 512:(tc + 1) * 512],
                                            wslot[cch % 2][:, k * 512 + ot * 128:k * 512 + (ot + 1) * 128],
                                            arO[:, k * TOK + tc * 512:k * TOK + (tc + 1) * 512],
                                            start=(k == 0), stop=(k == 15))
                    return last
                w2 = {"gpsimd": GL["gpsimd"]}
                if ("woev", o - 2) in pe_marks:
                    w2["vector"] = pe_marks[("woev", o - 2)]
                MM = P.step("tensor", wo_mm, waits=w2)
                st_marks[cch] = MM["tensor"]
                EV = P.step("vector", lambda e, o=o, slot=slot: e.tensor_tensor(Acc_t(o), Acc_t(o), slot, OP.add),
                            waits={"tensor": MM["tensor"]})
                pe_marks[("woev", o)] = EV["vector"]
                SX = P.step("gpsimd", lambda e, o=o: e.dma_start(out=x1T[o * 128:(o + 1) * 128, :], in_=Acc_t(o)),
                            waits={"vector": EV["vector"]}, inc=16)
        X1 = P.snap()

        psS = psM[0:1, 0:1024]
        psQ = psM[0:1, 1024:2048]
        pe0 = P.cnt["tensor"]
        for k in range(NKT):
            w = {"vector": pe_marks[("woev", 15)]}
            if k >= 1:
                w["tensor"] = pe0 + k
            SQ = P.step("scalar", lambda e, k=k: e.activation(scr[:, 0:TOK], Acc_t(k), AF.Square), waits=w)
            def mm_k(e, k=k):
                for c in range(2):
                    e.matmul(psS[0:1, c * 512:(c + 1) * 512], ones1[:],
                             Acc_t(k)[:, c * 512:(c + 1) * 512], start=(k == 0), stop=(k == NKT - 1))
                last = None
                for c in range(2):
                    last = e.matmul(psQ[0:1, c * 512:(c + 1) * 512], ones1[:],
                                    scr[:, c * 512:(c + 1) * 512], start=(k == 0), stop=(k == NKT - 1))
                return last
            P.step("tensor", mm_k, waits={"scalar": SQ["scalar"], "vector": VE_ONES["vector"]})
        PE_STATS = P.snap()
        def var_emit(e):
            e.tensor_scalar(stat2[0:1, 0:TOK], psS[0:1, :], 1.0 / D, None, OP.mult)
            e.tensor_scalar(stat2[0:1, TOK:2 * TOK], psQ[0:1, :], 1.0 / D, None, OP.mult)
            e.tensor_tensor(scr[0:1, 0:TOK], stat2[0:1, 0:TOK], stat2[0:1, 0:TOK], OP.mult)
            e.tensor_tensor(stat2[0:1, TOK:2 * TOK], stat2[0:1, TOK:2 * TOK], scr[0:1, 0:TOK], OP.subtract)
            return e.tensor_scalar(stat2[0:1, TOK:2 * TOK], stat2[0:1, TOK:2 * TOK], LN_EPS, None, OP.add)
        VV = P.step("vector", var_emit, waits={"tensor": PE_STATS["tensor"]})
        RS = P.step("scalar", lambda e: (e.activation(stat2[0:1, TOK:2 * TOK], stat2[0:1, TOK:2 * TOK], AF.Ln),
                                         e.activation(stat2[0:1, TOK:2 * TOK], stat2[0:1, TOK:2 * TOK], AF.Exp,
                                                      scale=-0.5))[-1],
                    waits={"vector": VV["vector"]})
        def bc2(e, off, dst):
            last = None
            for c in range(2):
                last = e.matmul(dst[:, c * 512:(c + 1) * 512], onesr[:],
                                stat2[0:1, off + c * 512:off + (c + 1) * 512], start=True, stop=True)
            return last
        B1 = P.step("tensor", lambda e: bc2(e, 0, psM_s[0]), waits={"scalar": RS["scalar"]})
        B1E = P.step("vector", lambda e: e.tensor_copy(stat2[:, 0:1024], psM_s[0]), waits={"tensor": B1["tensor"]})
        B2 = P.step("tensor", lambda e: bc2(e, TOK, psM_s[1]), waits={"vector": B1E["vector"]})
        B2E = P.step("vector", lambda e: e.tensor_copy(stat2[:, 1024:2048], psM_s[1]), waits={"tensor": B2["tensor"]})
        xn_marks = []
        for k in range(NKT):
            XM = P.step("vector", lambda e, k=k: (
                e.tensor_tensor(scr[:, 0:TOK], Acc_t(k), stat2[:, 0:1024], OP.subtract),
                e.tensor_tensor(arO[:, k * TOK:(k + 1) * TOK].bitcast(F32)[:, 0:0] if False else scr[:, 0:TOK],
                                scr[:, 0:TOK], stat2[:, 1024:2048], OP.mult),
                e.tensor_copy(arO[:, k * TOK:(k + 1) * TOK], scr[:, 0:TOK]))[-1],
                waits={"vector": B2E["vector"]})
            xn_marks.append(XM["vector"])
            P.step("gpsimd", lambda e, k=k: e.dma_start(out=xn1bT[k * 128:(k + 1) * 128, :],
                                                        in_=arO[:, k * TOK:(k + 1) * TOK]),
                   waits={"vector": XM["vector"]}, inc=16)
        XN1 = P.snap()

        nch = 9
        stg = [arP[:, 0:1024], arP[:, 1024:2048], arP[:, 2048:3072], arP[:, 3072:4096]]
        pm = {}
        for cch in range(nch):
            CW = 512 if cch < 8 else 288
            w = {"vector": xn_marks[-1]}
            if cch >= 2:
                w["tensor"] = pm[("mm", cch - 2, 7)]
            GL = P.step("gpsimd", lambda e, cch=cch, CW=CW: e.dma_start(
                out=wslot[cch % 2][:, 0:16 * CW].rearrange("p (k n) -> p k n", k=16),
                in_=winT[:, cch * 512:cch * 512 + CW].rearrange("(k p) n -> p k n", p=128)),
                waits=w, inc=16)
            for tt in range(8):
                slot = psM_s[tt % 2]
                def pj_mm(e, cch=cch, tt=tt, slot=slot, CW=CW):
                    last = None
                    for k in range(16):
                        last = e.matmul(slot[:, 0:CW], arO[:, k * TOK + 128 * tt:k * TOK + 128 * tt + 128],
                                        wslot[cch % 2][:, k * CW:(k + 1) * CW], start=(k == 0), stop=(k == 15))
                    return last
                w2 = {"gpsimd": GL["gpsimd"], "vector": xn_marks[-1]}
                if ("ev", cch, tt - 2) in pm:
                    w2["scalar"] = pm[("ev", cch, tt - 2)]
                if ("st", cch - 1, tt) in pm:
                    w2["gpsimd"] = max(w2["gpsimd"], pm[("st", cch - 1, tt)])
                MM = P.step("tensor", pj_mm, waits=w2)
                pm[("mm", cch, tt)] = MM["tensor"]
                sg = stg[tt % 4]
                wev = {"tensor": MM["tensor"]}
                prevkey = ("st", cch, tt - 4) if tt >= 4 else ("st", cch - 1, tt + 4)
                if prevkey in pm:
                    wev["gpsimd"] = pm[prevkey]
                EV = P.step("scalar", lambda e, slot=slot, sg=sg, CW=CW: e.activation(sg[:, 0:CW], slot[:, 0:CW], AF.Copy),
                            waits=wev)
                pm[("ev", cch, tt)] = EV["scalar"]
                if cch < 4:
                    dst = gate[128 * tt:128 * tt + 128, cch * 512:(cch + 1) * 512]
                else:
                    dst = hbcdt[128 * tt:128 * tt + 128, cch * 512 - 2048:cch * 512 - 2048 + CW]
                STp = P.step("gpsimd", lambda e, dst=dst, sg=sg, CW=CW: e.dma_start(out=dst, in_=sg[:, 0:CW]),
                             waits={"scalar": EV["scalar"]}, inc=16)
                pm[("st", cch, tt)] = STp["gpsimd"]
        P.self_wait("gpsimd", P.cnt["gpsimd"])
        P.self_wait("sync", P.cnt["sync"])
    return nc


def build_C(stop=99):
    nc = bass.Bass("TRN2", target_bir_lowering=False)
    hbcdtP = nc.dram_tensor("hbcdtP", [1152, 2336], BF16, kind="ExternalInput").ap()
    dtrawT = nc.dram_tensor("dtrawT", [32, TOK], F32, kind="ExternalInput").ap()
    convw = nc.dram_tensor("convw", [128, 72], F32, kind="ExternalInput").ap()
    convb = nc.dram_tensor("convb", [128, 18], F32, kind="ExternalInput").ap()
    dtvec = nc.dram_tensor("dtvec", [32, 4], F32, kind="ExternalInput").ap()
    lutrow = nc.dram_tensor("lutrow", [1, 16], F32, kind="ExternalInput").ap()
    gidxT = nc.dram_tensor("gidxT", [D, 768], I32, kind="ExternalInput").ap()
    uidxT = nc.dram_tensor("uidxT", [D, 768], I32, kind="ExternalInput").ap()
    didxT = nc.dram_tensor("didxT", [768, D], I32, kind="ExternalInput").ap()
    strict = nc.dram_tensor("strict", [128, 128], F32, kind="ExternalInput").ap()
    triincl = nc.dram_tensor("triincl", [128, 128], F32, kind="ExternalInput").ap()
    triu01 = nc.dram_tensor("triu01", [128, 128], BF16, kind="ExternalInput").ap()
    yintra = nc.dram_tensor("yintra", [TOK, D], BF16, kind="ExternalOutput").ap()
    xstok = nc.dram_tensor("xstok", [TOK, D], BF16, kind="ExternalOutput").ap()
    hloc = nc.dram_tensor("hloc", [8 * 128, D], BF16, kind="ExternalOutput").ap()
    wtok_o = nc.dram_tensor("wtok_o", [TOK, 32], F32, kind="ExternalOutput").ap()
    cfm_o = nc.dram_tensor("cfm_o", [128, TOK], BF16, kind="ExternalOutput").ap()
    segld = nc.dram_tensor("segld", [1, 32], F32, kind="ExternalOutput").ap()
    gW = nc.dram_tensor("gW", [D, 768], BF16, kind="ExternalOutput").ap()
    uW = nc.dram_tensor("uW", [D, 768], BF16, kind="ExternalOutput").ap()
    dW = nc.dram_tensor("dW", [768, D], BF16, kind="ExternalOutput").ap()

    NCH = 18
    with ExitStack() as ctx:
        arF = ctx.enter_context(nc.sbuf_tensor("arF", [128, 9 * 1152], BF16))
        arC2 = ctx.enter_context(nc.sbuf_tensor("arC2", [128, NCH * 1024], BF16))
        arXT = ctx.enter_context(nc.sbuf_tensor("arXT", [128, 16 * 1024], BF16))
        arYH = ctx.enter_context(nc.sbuf_tensor("arYH", [128, 2048], F32))
        arY = ctx.enter_context(nc.sbuf_tensor("arY", [128, 8 * 2048], BF16))
        arG = ctx.enter_context(nc.sbuf_tensor("arG", [128, 16 * 768], BF16))
        arBt = ctx.enter_context(nc.sbuf_tensor("arBt", [128, 8 * 128], BF16))
        sdt = ctx.enter_context(nc.sbuf_tensor("sdt", [32, 3 * 1024], F32))
        stok = ctx.enter_context(nc.sbuf_tensor("stok", [128, 1152], F32))
        consts = ctx.enter_context(nc.sbuf_tensor("consts", [128, 3 * 128 + 72 + 18 + 64], F32))
        lutb = ctx.enter_context(nc.sbuf_tensor("lutb", [128, 16], F32))
        onesr = ctx.enter_context(nc.sbuf_tensor("onesr", [1, 128], F32))
        idxt = ctx.enter_context(nc.sbuf_tensor("idxt", [128, 2 * 2048], BF16))
        scr = ctx.enter_context(nc.sbuf_tensor("scr", [128, 2048], F32))
        ident = ctx.enter_context(nc.sbuf_tensor("ident", [128, 128], BF16))
        identf = ctx.enter_context(nc.sbuf_tensor("identf", [32, 32], F32))
        identF = ctx.enter_context(nc.sbuf_tensor("identF", [128, 128], F32))
        psU = ctx.enter_context(nc.psum_tensor("psU", [128, 2048], F32))
        psV = ctx.enter_context(nc.psum_tensor("psV", [128, 1024], F32))
        psTB0 = ctx.enter_context(nc.psum_tensor("psTB0", [128, 128], BF16))
        psTB1 = ctx.enter_context(nc.psum_tensor("psTB1", [128, 128], BF16))
        psTBs = [psTB0, psTB1]
        s_sync = ctx.enter_context(nc.semaphore("s_sync"))
        s_gp = ctx.enter_context(nc.semaphore("s_gp"))
        s_pe = ctx.enter_context(nc.semaphore("s_pe"))
        s_ac = ctx.enter_context(nc.semaphore("s_ac"))
        s_ve = ctx.enter_context(nc.semaphore("s_ve"))
        block = ctx.enter_context(nc.Block())
        P = Prog(nc, block, dict(sync=s_sync, gpsimd=s_gp, tensor=s_pe, scalar=s_ac, vector=s_ve))

        CSTRICT = consts[:, 0:128]
        CTRI = consts[:, 128:256]
        CTRIU = consts[:, 256:384].bitcast(BF16)[:, 0:128]
        CW = consts[:, 384:456]
        CB = consts[:, 456:474]
        DT_RAW = sdt[:, 0:1024]
        DT_SP = sdt[:, 1024:2048]
        SC32 = sdt[:, 2048:3072]
        W_FM = sdt[:, 2048:3072]
        SC32b = sdt[:, 0:1024]

        ldf_marks = {}
        for t in range(9):
            LF = P.step("sync", lambda e, t=t: e.dma_start(
                out=arF[:, t * 1152:(t + 1) * 1152], in_=hbcdtP[:, t * 128:(t + 1) * 128], transpose=True), inc=16)
            ldf_marks[t] = LF["sync"]
        LD_F = P.snap()
        P.step("sync", lambda e: e.dma_start(out=DT_RAW[0:32, :], in_=dtrawT[:]), inc=16)
        LD_DT = P.snap()
        P.step("sync", lambda e: e.dma_start(out=CW, in_=convw[:]), inc=16)
        P.step("sync", lambda e: e.dma_start(out=CB, in_=convb[:]), inc=16)
        P.step("sync", lambda e: e.dma_start(out=consts[0:32, 474:478], in_=dtvec[:]), inc=16)
        P.step("sync", lambda e: e.dma_start(out=CSTRICT, in_=strict[:]), inc=16)
        P.step("sync", lambda e: e.dma_start(out=CTRI, in_=triincl[:]), inc=16)
        P.step("sync", lambda e: e.dma_start(out=consts[:, 256:320], in_=triu01[:].bitcast(F32)), inc=16)
        P.step("sync", lambda e: e.dma_start(out=lutb[0:1, :], in_=lutrow[:]), inc=16)
        LD_C = P.snap()
        P.step("vector", lambda e: e.memset(onesr[:], 1.0))
        ONES = P.snap()
        def id_emit(e):
            e.iota(scr[:, 0:128], pattern=[[1, 128]], base=0, channel_multiplier=0,
                   allow_small_or_imprecise_dtypes=True)
            return e.iota(scr[:, 128:256], pattern=[[0, 128]], base=0, channel_multiplier=1,
                          allow_small_or_imprecise_dtypes=True)
        IDG0 = P.step("gpsimd", id_emit)
        IDG = P.step("vector", lambda e: e.tensor_tensor(scr[:, 0:128], scr[:, 0:128], scr[:, 128:256], OP.is_equal),
                     waits={"gpsimd": IDG0["gpsimd"]})
        IDC = P.step("vector", lambda e: (e.tensor_copy(ident[:], scr[:, 0:128]),
                                          e.tensor_copy(identF[:], scr[:, 0:128]),
                                          e.tensor_copy(identf[:], scr[0:32, 0:32]))[-1])

        LB = P.step("tensor", lambda e: e.matmul(psU[:, 448:464], onesr[:], lutb[0:1, :], start=True, stop=True),
                    waits={"sync": LD_C["sync"], "vector": ONES["vector"]})
        LBE = P.step("vector", lambda e: e.tensor_copy(lutb[:], psU[:, 448:464]), waits={"tensor": LB["tensor"]})

        ghost_jobs = []
        ghost_jobs.append((gidxT, gW, 16, 768))
        ghost_jobs.append((uidxT, uW, 16, 768))
        ghost_jobs.append((didxT, dW, 6, 2048))
        gh_marks = {}
        slotw = {0: 768, 1: 768, 2: 2048}
        for jj, (srcd, outd, nt, wdt) in enumerate(ghost_jobs):
            eng = "vector"
            for t in range(nt):
                sl = idxt[:, (t % 2) * 2048:(t % 2) * 2048 + wdt]
                wld = {}
                if ("gsel", jj, t - 2) in gh_marks:
                    wld[eng] = gh_marks[("gsel", jj, t - 2)]
                rows = slice(t * 128, (t + 1) * 128)
                GL = P.step("gpsimd", lambda e, srcd=srcd, rows=rows, sl=sl, wdt=wdt: e.dma_start(
                    out=sl, in_=srcd[rows, 0:wdt]), waits=wld, inc=16)
                acc_sl = arG[:, 0:wdt] if eng == "vector" else arG[:, 8 * 768:8 * 768 + wdt]
                tmp_sl = arG[:, 4 * 768:4 * 768 + wdt] if eng == "vector" else arG[:, 12 * 768:12 * 768 + wdt]
                def gsel(e, sl=sl, acc_sl=acc_sl, tmp_sl=tmp_sl, wdt=wdt):
                    e.memset(acc_sl, 0.0)
                    for kk in range(16):
                        e.tensor_scalar(tmp_sl, sl, float(kk), lutb[:, kk:kk + 1], OP.is_equal, OP.mult)
                        e.tensor_tensor(acc_sl, acc_sl, tmp_sl, OP.add)
                    return e.tensor_copy(acc_sl, acc_sl)
                GS = P.step(eng, gsel, waits={"gpsimd": GL["gpsimd"], "vector": LBE["vector"]})
                gh_marks[("gsel", jj, t)] = GS[eng]
                GSt = P.step("gpsimd", lambda e, outd=outd, rows=rows, acc_sl=acc_sl, wdt=wdt: e.dma_start(
                    out=outd[rows, 0:wdt], in_=acc_sl), waits={eng: GS[eng]}, inc=16)
                gh_marks[("gst", jj, t)] = GSt["gpsimd"]
        GHOST = P.snap()
        if stop <= 1:
            P.self_wait("gpsimd", P.cnt["gpsimd"]); P.self_wait("sync", P.cnt["sync"]); return nc

        cv_marks = []
        for t in range(NCH):
            sl = t % 9
            def conv_t(e, t=t, sl=sl):
                fin = arF[:, sl * 1152:(sl + 1) * 1152]
                e.tensor_scalar(scr[:, 0:1024], fin[:, 0:1024], CW[:, t * 4 + 0:t * 4 + 1], None, OP.mult)
                for j in range(1, 4):
                    e.scalar_tensor_tensor(scr[:, 0:1024], fin[:, j:1024 + j], CW[:, t * 4 + j:t * 4 + j + 1],
                                           scr[:, 0:1024], OP.mult, OP.add)
                return e.tensor_copy(scr[:, 1024:2048], scr[:, 0:1024])
            VM = P.step("vector", conv_t, waits={"sync": ldf_marks[t], "scalar": P.cnt["scalar"],
                                                 "vector": P.cnt["vector"]})
            AM = P.step("scalar", lambda e, t=t: e.activation(
                arC2[:, t * 1024:(t + 1) * 1024], scr[:, 1024:2048], AF.Silu, bias=CB[:, t:t + 1]),
                waits={"vector": VM["vector"]})
            cv_marks.append(AM["scalar"])
            if t < 9:
                LF = P.step("sync", lambda e, t=t: e.dma_start(
                    out=arF[:, t * 1152:(t + 1) * 1152], in_=hbcdtP[:, (t + 9) * 128:(t + 10) * 128], transpose=True),
                    waits={"vector": VM["vector"]}, inc=16)
                ldf_marks[9 + t] = LF["sync"]
        CONV = P.snap()
        if stop <= 2:
            P.self_wait("gpsimd", P.cnt["gpsimd"]); P.self_wait("sync", P.cnt["sync"]); return nc

        P.step("gpsimd", lambda e: e.dma_start(out=cfm_o[:], in_=arC2[:, 17 * 1024:18 * 1024]),
               waits={"scalar": cv_marks[17]}, inc=16)
        if stop <= 2.5:
            P.self_wait("gpsimd", P.cnt["gpsimd"]); P.self_wait("sync", P.cnt["sync"]); return nc

        tr_marks = {}
        for kt in range(16 if stop > 2.7 else 1):
            for c8 in range(8):
                pslot = psTBs[c8 % 2][:]
                wtr = {"scalar": cv_marks[kt], "vector": IDC["vector"]}
                if ("slot", c8 % 2) in tr_marks:
                    wtr["vector"] = max(wtr["vector"], tr_marks[("slot", c8 % 2)])
                TR = P.step("tensor", lambda e, kt=kt, c8=c8, pslot=pslot: e.transpose(
                    pslot, arC2[:, kt * 1024 + c8 * 128:kt * 1024 + (c8 + 1) * 128], ident[:]), waits=wtr)
                TE = P.step("vector", lambda e, kt=kt, c8=c8, pslot=pslot: e.tensor_copy(
                    arXT[:, (c8 * 16 + kt) * 128:(c8 * 16 + kt + 1) * 128], pslot),
                    waits={"tensor": TR["tensor"]})
                tr_marks[("trev", kt, c8)] = TE["vector"]
                tr_marks[("slot", c8 % 2)] = TE["vector"]
        XT = P.snap()
        for c8 in range(8 if stop > 2.7 else 0):
            P.step("gpsimd", lambda e, c8=c8: e.dma_start(
                out=xstok[c8 * 128:(c8 + 1) * 128, :], in_=arXT[:, c8 * 2048:(c8 + 1) * 2048]),
                waits={"vector": tr_marks[("trev", 15, c8)]}, inc=16)
        if stop <= 2.8:
            P.self_wait("gpsimd", P.cnt["gpsimd"]); P.self_wait("sync", P.cnt["sync"]); return nc
        bt_marks = {}
        for c8 in range(8):
            pslot = psTBs[c8 % 2][:]
            wtr = {"scalar": cv_marks[16], "tensor": XT["tensor"], "vector": tr_marks[("trev", 15, 7)]}
            if ("btev", c8 - 2) in bt_marks:
                wtr["vector"] = max(wtr["vector"], bt_marks[("btev", c8 - 2)])
            TR = P.step("tensor", lambda e, c8=c8, pslot=pslot: e.transpose(
                pslot, arC2[:, 16 * 1024 + c8 * 128:16 * 1024 + (c8 + 1) * 128], ident[:]), waits=wtr)
            TE = P.step("vector", lambda e, c8=c8, pslot=pslot: e.tensor_copy(
                arBt[:, c8 * 128:(c8 + 1) * 128], pslot), waits={"tensor": TR["tensor"]})
            bt_marks[("btev", c8)] = TE["vector"]
        BT = P.snap()
        if stop <= 3:
            P.self_wait("gpsimd", P.cnt["gpsimd"]); P.self_wait("sync", P.cnt["sync"]); return nc

        wdt0 = {"sync": max(LD_C["sync"], LD_DT["sync"])}
        D1 = P.step("scalar", lambda e: e.activation(SC32[0:32, :], DT_RAW[0:32, :], AF.Abs,
                                                     bias=consts[0:32, 474:475]), waits=wdt0)
        P.self_wait("scalar", D1["scalar"])
        D2 = P.step("scalar", lambda e: e.activation(DT_SP[0:32, :], SC32[0:32, :], AF.Exp, scale=-1.0))
        P.self_wait("scalar", D2["scalar"])
        D3 = P.step("scalar", lambda e: e.activation(SC32[0:32, :], DT_SP[0:32, :], AF.Ln, bias=1.0))
        P.self_wait("scalar", D3["scalar"])
        D4 = P.step("scalar", lambda e: e.activation(DT_SP[0:32, :], DT_RAW[0:32, :], AF.Relu,
                                                     bias=consts[0:32, 474:475]))
        P.self_wait("scalar", D4["scalar"])
        DC = P.step("scalar", lambda e: e.activation(consts[0:32, 478:479], consts[0:32, 475:476], AF.Exp))
        def dt_vec(e):
            e.tensor_tensor(DT_SP[0:32, :], DT_SP[0:32, :], SC32[0:32, :], OP.add)
            e.tensor_scalar(SC32b[0:32, :], DT_SP[0:32, :], consts[0:32, 478:479], None, OP.mult)
            return e.tensor_scalar(SC32b[0:32, :], SC32b[0:32, :], -1.0, None, OP.mult)
        DV = P.step("vector", dt_vec, waits={"scalar": DC["scalar"]})
        P.self_wait("vector", DV["vector"])
        WC = DV
        dtk_marks = {}
        for c8 in range(8):
            pslot = psV[:, (c8 % 2) * 256:(c8 % 2) * 256 + 256]
            wtr = {"vector": WC["vector"]}
            if ("dtk", c8 - 2) in dtk_marks:
                wtr["vector"] = max(wtr["vector"], dtk_marks[("dtk", c8 - 2)])
            TRm = P.step("tensor", lambda e, c8=c8, pslot=pslot: e.matmul(
                pslot[:, 0:32], DT_SP[0:32, c8 * 128:(c8 + 1) * 128], identf[0:32, 0:32],
                start=True, stop=True), waits=wtr)
            TR2 = P.step("tensor", lambda e, c8=c8, pslot=pslot: e.matmul(
                pslot[:, 32:64], SC32b[0:32, c8 * 128:(c8 + 1) * 128], identf[0:32, 0:32],
                start=True, stop=True))
            TE = P.step("vector", lambda e, c8=c8, pslot=pslot: (
                e.tensor_copy(stok[:, c8 * 32:(c8 + 1) * 32], pslot[:, 0:32]),
                e.tensor_copy(stok[:, 544 + c8 * 32:544 + (c8 + 1) * 32], pslot[:, 32:64]))[-1],
                waits={"tensor": TR2["tensor"]})
            dtk_marks[("dtk", c8)] = TE["vector"]
        WCT = P.step("tensor", lambda e: e.matmul(psV[:, 512:768], CTRI, stok[:, 544:800],
                                                  start=True, stop=True),
                     waits={"vector": dtk_marks[("dtk", 7)], "sync": LD_C["sync"]})
        WCE = P.step("vector", lambda e: e.tensor_copy(stok[:, 256:512], psV[:, 512:768]),
                     waits={"tensor": WCT["tensor"]})
        P.self_wait("vector", WCE["vector"])
        dtk_marks[("dtk", 7)] = WCE["vector"]
        if stop <= 4:
            P.self_wait("gpsimd", P.cnt["gpsimd"]); P.self_wait("sync", P.cnt["sync"]); return nc
        DBGW = 256
        import os as _os
        DBGW = int(_os.environ.get("DBGW", "256"))
        P.step("gpsimd", lambda e: e.dma_start(
            out=wtok_o.rearrange("(c p) h -> p c h", p=128),
            in_=stok[:, DBGW:DBGW + 256].rearrange("p (c h) -> p c h", h=32)),
            waits={"vector": dtk_marks[("dtk", 7)]}, inc=16)

        arS = arG
        scrb = scr[:].bitcast(BF16)
        GMSK = scrb[:, 2048:2176]
        SSL = scrb[:, 2176:2304]
        XP = scrb[:, 2304:2368]
        XE = scrb[:, 2368:2432]
        psG = psU[:, 0:128]
        psD = psU[:, 128:256]
        psY = psU[:, 256:320]
        psH = psU[:, 320:384]
        psT = psV[:, 256:512]
        H_t = arYH

        HIN = P.step("vector", lambda e: e.memset(H_t[:], 0.0))
        mk_prev = {}
        for c8 in range(8):
            cs, ce = c8 * 128, (c8 + 1) * 128
            WR = P.step("gpsimd", lambda e, c8=c8: e.dma_start(
                out=stok[0:1, 1088:1120], in_=stok[127:128, 256 + c8 * 32:256 + (c8 + 1) * 32]),
                waits={"vector": max(dtk_marks[("dtk", 7)], P.cnt["vector"])}, inc=16)
            BCW = P.step("tensor", lambda e: e.matmul(psT[:, 96:128], onesr[:], stok[0:1, 1088:1120],
                                                      start=True, stop=True),
                         waits={"gpsimd": WR["gpsimd"], "vector": ONES["vector"]})
            BCWE = P.step("vector", lambda e: e.tensor_copy(stok[:, 1056:1088], psT[:, 96:128]),
                          waits={"tensor": BCW["tensor"]})
            if c8 == 0:
                SLD = P.step("vector", lambda e: e.tensor_copy(stok[0:1, 1120:1152], stok[0:1, 1088:1120]),
                             waits={"gpsimd": WR["gpsimd"]})
            else:
                SLD = P.step("vector", lambda e: e.tensor_tensor(stok[0:1, 1120:1152], stok[0:1, 1120:1152],
                                                                 stok[0:1, 1088:1120], OP.add),
                             waits={"gpsimd": WR["gpsimd"]})
            if c8 == 7:
                P.self_wait("vector", SLD["vector"])
                P.step("gpsimd", lambda e: e.dma_start(out=segld[:], in_=stok[0:1, 1120:1152]),
                       waits={"vector": SLD["vector"]}, inc=16)
            ESV = P.step("vector", lambda e, c8=c8: e.tensor_tensor(
                scr[:, 384:416], stok[:, 1056:1088], stok[:, 256 + c8 * 32:256 + (c8 + 1) * 32], OP.subtract),
                waits={"vector": BCWE["vector"]})
            ESA = P.step("scalar", lambda e: e.activation(scr[:, 384:416], scr[:, 384:416], AF.Exp),
                         waits={"vector": ESV["vector"]})
            DTE = P.step("vector", lambda e, c8=c8: e.tensor_tensor(
                stok[:, 800 + c8 * 32:800 + (c8 + 1) * 32], stok[:, c8 * 32:(c8 + 1) * 32],
                scr[:, 384:416], OP.mult), waits={"scalar": ESA["scalar"]})
            DCE2 = P.step("scalar", lambda e: e.activation(stok[:, 512:544], stok[:, 1056:1088], AF.Exp),
                          waits={"vector": BCWE["vector"]})
            EVT = P.step("vector", lambda e: e.engine_nop() if False else e.tensor_copy(scr[:, 416:417], scr[:, 416:417]),
                         waits={"scalar": DCE2["scalar"], "vector": DTE["vector"]})
            GM = P.step("tensor", lambda e, cs=cs, ce=ce: e.matmul(
                psG, arC2[:, 16 * 1024 + cs:16 * 1024 + ce], arC2[:, 17 * 1024 + cs:17 * 1024 + ce],
                start=True, stop=True),
                waits={"scalar": cv_marks[17], "vector": EVT["vector"]})
            GE = P.step("vector", lambda e: e.tensor_tensor(GMSK, psG, CTRIU, OP.mult),
                        waits={"tensor": GM["tensor"], "sync": LD_C["sync"]})
            for h in range(32):
                wz = {"vector": GE["vector"]}
                if ("dmm", h - 1) in mk_prev:
                    wz["tensor"] = mk_prev[("dmm", h - 1)]
                ZV = P.step("vector", lambda e, c8=c8, h=h: e.tensor_scalar(
                    scr[:, 0:128], CSTRICT, stok[:, 544 + c8 * 32 + h:544 + c8 * 32 + h + 1], None, OP.mult),
                    waits=wz)
                wd = {"vector": ZV["vector"]}
                if ("dexp", h - 1) in mk_prev:
                    wd["scalar"] = mk_prev[("dexp", h - 1)]
                DM = P.step("tensor", lambda e: e.matmul(psD, scr[:, 0:128], CTRI, start=True, stop=True), waits=wd)
                mk_prev[("dmm", h)] = DM["tensor"]
                wde = {"tensor": DM["tensor"]}
                if ("smul", h - 1) in mk_prev:
                    wde["vector"] = mk_prev[("smul", h - 1)]
                DE = P.step("scalar", lambda e: e.activation(scr[:, 128:256], psD, AF.Exp), waits=wde)
                mk_prev[("dexp", h)] = DE["scalar"]
                wsm = {"scalar": DE["scalar"]}
                if ("ymm", h - 1) in mk_prev:
                    wsm["tensor"] = max(mk_prev[("ymm", h - 1)], wsm.get("tensor", 0))
                SM = P.step("vector", lambda e: e.tensor_tensor(SSL, GMSK, scr[:, 128:256], OP.mult), waits=wsm)
                mk_prev[("smul", h)] = SM["vector"]
                XPS = P.step("vector", lambda e, c8=c8, h=h: e.tensor_scalar(
                    XP, arXT[:, c8 * 2048 + h * 64:c8 * 2048 + (h + 1) * 64],
                    stok[:, c8 * 32 + h:c8 * 32 + h + 1], None, OP.mult),
                    waits={"vector": tr_marks[("trev", 15, c8)]})
                wym = {"vector": max(SM["vector"], XPS["vector"])}
                if ("yev", h - 1) in mk_prev:
                    wym["vector"] = max(wym["vector"], 0)
                    wym2 = mk_prev[("yev", h - 1)]
                else:
                    wym2 = None
                def ymm(e):
                    return e.matmul(psY, SSL, XP, start=True, stop=True)
                YM = P.step("tensor", ymm, waits=wym)
                mk_prev[("ymm", h)] = YM["tensor"]
                YE = P.step("vector", lambda e, c8=c8, h=h: e.tensor_copy(
                    arY[:, c8 * 2048 + h * 64:c8 * 2048 + (h + 1) * 64], psY), waits={"tensor": YM["tensor"]})
                mk_prev[("yev", h)] = YE["vector"]
                XES = P.step("vector", lambda e, c8=c8, h=h: e.tensor_scalar(
                    XE, arXT[:, c8 * 2048 + h * 64:c8 * 2048 + (h + 1) * 64],
                    stok[:, 800 + c8 * 32 + h:800 + c8 * 32 + h + 1], None, OP.mult))
                whm = {"vector": XES["vector"], "scalar": cv_marks[16] if False else P.cnt["scalar"]}
                whm = {"vector": max(XES["vector"], bt_marks[("btev", 7)])}
                if ("hstt", h - 1) in mk_prev:
                    whm["vector"] = max(whm["vector"], mk_prev[("hstt", h - 1)])
                HM = P.step("tensor", lambda e, c8=c8: e.matmul(psH, arBt[:, c8 * 128:(c8 + 1) * 128], XE,
                                                               start=True, stop=True), waits=whm)
                HS = P.step("vector", lambda e, h=h: e.scalar_tensor_tensor(
                    H_t[:, h * 64:(h + 1) * 64], H_t[:, h * 64:(h + 1) * 64], stok[:, 512 + h:513 + h],
                    psH, OP.mult, OP.add), waits={"tensor": HM["tensor"], "vector": HIN["vector"]})
                mk_prev[("hstt", h)] = HS["vector"]
            P.step("gpsimd", lambda e, c8=c8: e.dma_start(
                out=yintra[c8 * 128:(c8 + 1) * 128, :], in_=arY[:, c8 * 2048:(c8 + 1) * 2048]),
                waits={"vector": mk_prev[("yev", 31)]}, inc=16)
            P.step("gpsimd", lambda e, c8=c8: e.dma_start(
                out=hloc[c8 * 128:(c8 + 1) * 128, :], in_=H_t[:]),
                waits={"vector": mk_prev[("hstt", 31)]}, inc=16)
            mk_prev = {}
        P.self_wait("gpsimd", P.cnt["gpsimd"])
        P.self_wait("sync", P.cnt["sync"])
    return nc


def build_D(stop=99):
    nc = bass.Bass("TRN2", target_bir_lowering=False)
    x1T = nc.dram_tensor("x1T", [D, TOK], F32, kind="ExternalInput").ap()
    xn1bT = nc.dram_tensor("xn1bT", [D, TOK], BF16, kind="ExternalInput").ap()
    gate = nc.dram_tensor("gate", [TOK, D], BF16, kind="ExternalInput").ap()
    yintra = nc.dram_tensor("yintra", [TOK, D], BF16, kind="ExternalInput").ap()
    xstok = nc.dram_tensor("xstok", [TOK, D], BF16, kind="ExternalInput").ap()
    cfm = nc.dram_tensor("cfm", [128, TOK], BF16, kind="ExternalInput").ap()
    wtok = nc.dram_tensor("wtok", [TOK, 32], F32, kind="ExternalInput").ap()
    hlocin = nc.dram_tensor("hlocin", [8 * 128, D], BF16, kind="ExternalInput").ap()
    hli4 = nc.dram_tensor("hli4", [4 * 128, D], BF16, kind="ExternalInput").ap()
    logd4 = nc.dram_tensor("logd4", [4, 32], F32, kind="ExternalInput").ap()
    mask2T = nc.dram_tensor("mask2T", [4, 128], F32, kind="ExternalInput").ap()
    mseg = nc.dram_tensor("mseg", [4, 1], F32, kind="ExternalInput").ap()
    dparam = nc.dram_tensor("dparam", [1, 32], F32, kind="ExternalInput").ap()
    normw = nc.dram_tensor("normw", [128, 16], F32, kind="ExternalInput").ap()
    woutT = nc.dram_tensor("woutT", [D, D], F32, kind="ExternalInput").ap()
    gWall = nc.dram_tensor("gWall", [D, 6144], BF16, kind="ExternalInput").ap()
    uWall = nc.dram_tensor("uWall", [D, 6144], BF16, kind="ExternalInput").ap()
    dWall = nc.dram_tensor("dWall", [6144, D], BF16, kind="ExternalInput").ap()
    outT = nc.dram_tensor("outT", [D, TOK], F32, kind="ExternalOutput").ap()

    NO = 43
    with ExitStack() as ctx:
        arX = ctx.enter_context(nc.sbuf_tensor("arX", [128, NKT * TOK], F32))
        arB2 = ctx.enter_context(nc.sbuf_tensor("arB2", [128, 66560], BF16))
        scr = ctx.enter_context(nc.sbuf_tensor("scr", [128, 2048], F32))
        stat = ctx.enter_context(nc.sbuf_tensor("stat", [128, 512], F32))
        onesr = ctx.enter_context(nc.sbuf_tensor("onesr", [1, 128], F32))
        identb_t = ctx.enter_context(nc.sbuf_tensor("identb_t", [128, 128], BF16))
        wtk = ctx.enter_context(nc.sbuf_tensor("wtk", [128, 256 + 64], F32))
        small = ctx.enter_context(nc.sbuf_tensor("small", [4, 96], F32))
        m2s = ctx.enter_context(nc.sbuf_tensor("m2s", [4, 128], F32))
        psM = ctx.enter_context(nc.psum_tensor("psM", [128, 2048], F32))
        psAux = ctx.enter_context(nc.psum_tensor("psAux", [128, 1024], F32))
        psTB0 = ctx.enter_context(nc.psum_tensor("psTB0", [128, 128], BF16))
        psTB1 = ctx.enter_context(nc.psum_tensor("psTB1", [128, 128], BF16))
        s_sync = ctx.enter_context(nc.semaphore("s_sync"))
        s_gp = ctx.enter_context(nc.semaphore("s_gp"))
        s_pe = ctx.enter_context(nc.semaphore("s_pe"))
        s_ac = ctx.enter_context(nc.semaphore("s_ac"))
        s_ve = ctx.enter_context(nc.semaphore("s_ve"))
        block = ctx.enter_context(nc.Block())
        P = Prog(nc, block, dict(sync=s_sync, gpsimd=s_gp, tensor=s_pe, scalar=s_ac, vector=s_ve))

        psM_s = [psM[:, 0:1024], psM[:, 1024:2048]]
        psY = psAux[:, 0:64]
        psBC = psAux[:, 64:96]
        psLW = psAux[:, 96:128]
        XN = lambda k: arB2[:, k * TOK:(k + 1) * TOK]
        def H_t(i): return arB2[:, 16384 + i * 512:16384 + (i + 1) * 512]
        wsl = [arB2[:, 38400:42496], arB2[:, 42496:46592]]
        GATE = lambda tt: arB2[:, tt * 2048:(tt + 1) * 2048]
        YG = lambda tt: arB2[:, 16384 + tt * 2048:16384 + (tt + 1) * 2048]
        HLI = [arB2[:, 32768:34816], arB2[:, 34816:36864]]
        GFM = lambda k: arB2[:, 36864 + k * TOK:36864 + (k + 1) * TOK]
        WSL2 = [arB2[:, 53248:57344], arB2[:, 57344:61440]]
        HIN = arB2[:, 61440:63488]
        cfms = arB2[:, 65536:66560]

        def Xt(k): return arX[:, k * TOK:(k + 1) * TOK]

        for k in range(NKT):
            P.step("sync", lambda e, k=k: e.dma_start(out=Xt(k), in_=x1T[k * 128:(k + 1) * 128, :]), inc=16)
        for k in range(NKT):
            P.step("sync", lambda e, k=k: e.dma_start(out=XN(k), in_=xn1bT[k * 128:(k + 1) * 128, :]), inc=16)
        LD0 = P.snap()
        P.step("vector", lambda e: e.memset(onesr[:], 1.0))
        ONES = P.snap()
        if stop <= 0.3:
            P.self_wait("gpsimd", P.cnt["gpsimd"]); P.self_wait("sync", P.cnt["sync"]); return nc

        gm = {}
        for th in range(2):
            tc = slice(th * 512, (th + 1) * 512)
            for c4 in range((NO + 1) // 2):
                ots = list(range(c4 * 2, min(c4 * 2 + 2, NO)))
                w = {"sync": LD0["sync"]}
                if ("humul", th, c4 - 1) in gm:
                    w["vector"] = gm[("humul", th, c4 - 1)]
                elif th == 1 and c4 == 0:
                    w["vector"] = gm[("dev", 0, 15, 5)]
                GL = P.step("gpsimd", lambda e, c4=c4, ots=ots: e.dma_start(
                    out=wsl[0][:, 0:len(ots) * 128 * 16].rearrange("p (k n) -> p k n", k=16),
                    in_=gWall[:, c4 * 256:c4 * 256 + len(ots) * 128].rearrange("(k p) n -> p k n", p=128)),
                    waits=w, inc=16)
                UL = P.step("gpsimd", lambda e, c4=c4, ots=ots: e.dma_start(
                    out=wsl[1][:, 0:len(ots) * 128 * 16].rearrange("p (k n) -> p k n", k=16),
                    in_=uWall[:, c4 * 256:c4 * 256 + len(ots) * 128].rearrange("(k p) n -> p k n", p=128)), inc=16)
                for i, o in enumerate(ots):
                    NW = len(ots) * 128
                    def gmm(e, i=i, tc=tc, NW=NW):
                        last = None
                        for k in range(16):
                            last = e.matmul(psM_s[0][:, 0:512], wsl[0][:, k * NW + i * 128:k * NW + (i + 1) * 128],
                                            XN(k)[:, tc], start=(k == 0), stop=(k == 15))
                        return last
                    def umm(e, i=i, tc=tc, NW=NW):
                        last = None
                        for k in range(16):
                            last = e.matmul(psM_s[1][:, 0:512], wsl[1][:, k * NW + i * 128:k * NW + (i + 1) * 128],
                                            XN(k)[:, tc], start=(k == 0), stop=(k == 15))
                        return last
                    w2 = {"gpsimd": UL["gpsimd"]}
                    if ("humul", th, c4 - 1) in gm and i == 0:
                        w2["vector"] = gm[("humul", th, c4 - 1)]
                    if i >= 1:
                        w2["vector"] = gm[("humul2", th, o - 1)]
                    MMg = P.step("tensor", gmm, waits=w2)
                    MMu = P.step("tensor", umm)
                    SIL = P.step("scalar", lambda e: e.activation(scr[:, 0:512], psM_s[0][:, 0:512], AF.Silu),
                                 waits={"tensor": MMu["tensor"]})
                    HU = P.step("vector", lambda e, o=o: e.tensor_tensor(H_t(o), scr[:, 0:512], psM_s[1][:, 0:512],
                                                                         OP.mult),
                                waits={"scalar": SIL["scalar"]})
                    gm[("humul2", th, o)] = HU["vector"]
                gm[("humul", th, c4)] = HU["vector"]
            if stop <= 0.6:
                continue
            for o2 in range(16 if stop > 0.8 else 1):
                kcs = [list(range(kc * 8, min(kc * 8 + 8, NO))) for kc in range(6 if stop > 0.7 else 1)]
                for kc, ks in enumerate(kcs):
                    w = {"vector": gm[("humul", th, 21)]}
                    if ("wsl2_user", kc % 2) in gm:
                        w["tensor"] = gm[("wsl2_user", kc % 2)]
                    DL = P.step("gpsimd", lambda e, o2=o2, ks=ks, kc=kc: e.dma_start(
                        out=WSL2[kc % 2][:, 0:len(ks) * 128].rearrange("p (k n) -> p k n", k=len(ks)),
                        in_=dWall[ks[0] * 128:(ks[-1] + 1) * 128, o2 * 128:(o2 + 1) * 128]
                            .rearrange("(k p) n -> p k n", p=128)),
                        waits=w, inc=16)
                    def dmm(e, ks=ks, kc=kc, th=th):
                        last = None
                        for j, k in enumerate(ks):
                            last = e.matmul(psM_s[0][:, 512:1024], WSL2[kc % 2][:, j * 128:(j + 1) * 128],
                                            H_t(k)[:, 0:512], start=(j == 0), stop=(j == len(ks) - 1))
                        return last
                    wmm = {"gpsimd": DL["gpsimd"]}
                    if ("dev_last",) in gm:
                        wmm["vector"] = gm[("dev_last",)]
                    DM = P.step("tensor", dmm, waits=wmm)
                    gm[("dwl", th, o2, kc)] = DM["tensor"]
                    gm[("wsl2_user", kc % 2)] = DM["tensor"]
                    DE = P.step("vector", lambda e, o2=o2, th=th: e.tensor_tensor(
                        Xt(o2)[:, th * 512:(th + 1) * 512], Xt(o2)[:, th * 512:(th + 1) * 512],
                        psM_s[0][:, 512:1024], OP.add), waits={"tensor": DM["tensor"], "sync": LD0["sync"]})
                    gm[("dev", th, o2, kc)] = DE["vector"]
                    gm[("dev_last",)] = DE["vector"]
        GHOST = P.snap()
        if stop <= 1:
            P.self_wait("gpsimd", P.cnt["gpsimd"]); P.self_wait("sync", P.cnt["sync"]); return nc

        CL = []
        CL.append(P.step("sync", lambda e: e.dma_start(out=small[:, 0:32], in_=logd4[:]), inc=16))
        CL.append(P.step("sync", lambda e: e.dma_start(out=m2s[:], in_=mask2T[:]), inc=16))
        CL.append(P.step("sync", lambda e: e.dma_start(out=small[:, 36:37], in_=mseg[:]), inc=16))
        CL.append(P.step("sync", lambda e: e.dma_start(out=stat[0:1, 32:64], in_=dparam[:]), inc=16))
        CL.append(P.step("sync", lambda e: e.dma_start(out=cfms[:], in_=cfm[:]), inc=16))
        CL.append(P.step("sync", lambda e: e.dma_start(
            out=wtk[:, 0:256].rearrange("p (c h) -> p c h", h=32),
            in_=wtok[:].rearrange("(c p) h -> p c h", p=128)), inc=16))
        LDC = P.snap()
        LW = P.step("tensor", lambda e: e.matmul(psLW, m2s[:], small[:, 0:32], start=True, stop=True),
                    waits={"sync": LDC["sync"]})
        LWE = P.step("vector", lambda e: e.tensor_copy(small[:, 37:69], psLW[0:4, :]),
                     waits={"tensor": LW["tensor"]})
        P.self_wait("vector", LWE["vector"])
        W4A = P.step("scalar", lambda e: e.activation(small[:, 37:69], small[:, 37:69], AF.Exp),
                     waits={"vector": LWE["vector"]})
        W4V = P.step("vector", lambda e: e.tensor_scalar(small[:, 37:69], small[:, 37:69], small[:, 36:37],
                                                         None, OP.mult), waits={"scalar": W4A["scalar"]})
        P.self_wait("vector", W4V["vector"])
        wb_marks = []
        for j in range(4):
            WJ = P.step("gpsimd", lambda e, j=j: e.dma_start(out=stat[0:1, 64 + j * 32:64 + (j + 1) * 32],
                                                             in_=small[j:j + 1, 37:69]),
                        waits={"vector": W4V["vector"]}, inc=16)
            BC = P.step("tensor", lambda e, j=j: e.matmul(psBC, onesr[:], stat[0:1, 64 + j * 32:64 + (j + 1) * 32],
                                                          start=True, stop=True),
                        waits={"gpsimd": WJ["gpsimd"], "vector": ONES["vector"]})
            BE = P.step("vector", lambda e, j=j: e.tensor_copy(stat[:, 192 + j * 32:192 + (j + 1) * 32], psBC),
                        waits={"tensor": BC["tensor"]})
            wb_marks.append(BE["vector"])
        DPB = P.step("tensor", lambda e: e.matmul(psBC, onesr[:], stat[0:1, 32:64], start=True, stop=True),
                     waits={"sync": LDC["sync"], "vector": ONES["vector"]})
        DPE = P.step("vector", lambda e: e.tensor_copy(stat[:, 320:352], psBC), waits={"tensor": DPB["tensor"]})
        HI0 = P.step("vector", lambda e: e.memset(HIN, 0.0), waits={"vector": gm[("dev", 1, 15, 5)]})
        him = HI0["vector"]
        for j in range(4):
            HL = P.step("sync", lambda e, j=j: e.dma_start(out=HLI[j % 2], in_=hli4[j * 128:(j + 1) * 128, :]),
                        waits={"vector": him}, inc=16)
            for h in range(32):
                HS = P.step("vector", lambda e, j=j, h=h: e.scalar_tensor_tensor(
                    HIN[:, h * 64:(h + 1) * 64], HLI[j % 2][:, h * 64:(h + 1) * 64],
                    stat[:, 192 + j * 32 + h:192 + j * 32 + h + 1], HIN[:, h * 64:(h + 1) * 64],
                    OP.mult, OP.add), waits={"sync": HL["sync"], "vector": wb_marks[j]})
            him = HS["vector"]
        HIND = P.snap()
        if stop <= 2:
            P.self_wait("gpsimd", P.cnt["gpsimd"]); P.self_wait("sync", P.cnt["sync"]); return nc

        LAM0 = P.step("vector", lambda e: e.memset(stat[0:1, 96:128], 0.0))
        lam_mark = LAM0["vector"]
        ym = {}
        for c8 in range(8):
            EW = P.step("scalar", lambda e, c8=c8: e.activation(stat[:, 352 + (c8 % 2) * 32:352 + (c8 % 2) * 32 + 32],
                                                                wtk[:, c8 * 32:(c8 + 1) * 32], AF.Exp),
                        waits={"sync": LDC["sync"], "vector": ym.get(("yg2", c8 - 2, 31), 0)})
            LA = P.step("scalar", lambda e: e.activation(stat[0:1, 128:160], stat[0:1, 96:128], AF.Exp),
                        waits={"vector": lam_mark})
            LB = P.step("tensor", lambda e: e.matmul(psBC, onesr[:], stat[0:1, 128:160], start=True, stop=True),
                        waits={"scalar": LA["scalar"], "vector": ONES["vector"]})
            LBE = P.step("vector", lambda e, c8=c8: e.tensor_copy(stat[:, 416 + (c8 % 2) * 64:448 + (c8 % 2) * 64], psBC),
                         waits={"tensor": LB["tensor"]})
            WEX = P.step("gpsimd", lambda e, c8=c8: e.dma_start(out=stat[0:1, 160:192],
                                                                in_=wtk[127:128, c8 * 32:(c8 + 1) * 32]),
                         waits={"sync": LDC["sync"], "vector": wb_marks[3]}, inc=16)
            LUP = P.step("vector", lambda e: e.tensor_tensor(stat[0:1, 96:128], stat[0:1, 96:128],
                                                             stat[0:1, 160:192], OP.add),
                         waits={"gpsimd": WEX["gpsimd"], "vector": LBE["vector"]})
            lam_mark = LUP["vector"]
            hsl = HLI[c8 % 2]
            if c8 == 0:
                HLD = P.step("vector", lambda e, hsl=hsl: e.memset(hsl, 0.0),
                             waits={"vector": max(HIND["vector"], ym.get(("yg2", c8 - 2, 31), 0))})
                hld_w = {"vector": HLD["vector"]}
            else:
                HLD = P.step("sync", lambda e, c8=c8, hsl=hsl: e.dma_start(
                    out=hsl, in_=hlocin[(c8 - 1) * 128:c8 * 128, :]),
                    waits={"vector": max(HIND["vector"], ym.get(("yg2", c8 - 2, 31), 0))}, inc=16)
                hld_w = {"sync": HLD["sync"]}
            XSL = P.step("sync", lambda e, c8=c8: e.dma_start(out=arB2[:, 63488:63488 + 2048],
                                                              in_=xstok[c8 * 128:(c8 + 1) * 128, :]),
                         waits={"vector": ym.get(("yg2", c8 - 1, 31), 0)}, inc=16)
            YL = P.step("sync", lambda e, c8=c8: e.dma_start(out=YG(c8), in_=yintra[c8 * 128:(c8 + 1) * 128, :]),
                        inc=16)
            heff_m = 0
            for h in range(32):
                HS = P.step("vector", lambda e, h=h, hsl=hsl, c8=c8: e.scalar_tensor_tensor(
                    hsl[:, h * 64:(h + 1) * 64], HIN[:, h * 64:(h + 1) * 64],
                    stat[:, 416 + (c8 % 2) * 64 + h:417 + (c8 % 2) * 64 + h], hsl[:, h * 64:(h + 1) * 64],
                    OP.mult, OP.add), waits=dict(list(hld_w.items()) + [("vector", LBE["vector"])]))
                heff_m = HS["vector"]
                wy = {"vector": heff_m, "sync": max(LDC["sync"], CL[4]["sync"])}
                if ("ymm", c8, h - 1) in ym:
                    wy["vector"] = max(wy["vector"], ym[("yg2", c8, h - 1)])
                YM = P.step("tensor", lambda e, c8=c8, h=h, hsl=hsl: e.matmul(
                    psY, cfms[:, c8 * 128:(c8 + 1) * 128], hsl[:, h * 64:(h + 1) * 64],
                    start=True, stop=True), waits=wy)
                ym[("ymm", c8, h)] = YM["tensor"]
                YE = P.step("vector", lambda e, c8=c8, h=h: e.scalar_tensor_tensor(
                    YG(c8)[:, h * 64:(h + 1) * 64], psY, stat[:, 352 + (c8 % 2) * 32 + h:353 + (c8 % 2) * 32 + h],
                    YG(c8)[:, h * 64:(h + 1) * 64], OP.mult, OP.add),
                    waits={"tensor": YM["tensor"], "sync": YL["sync"], "scalar": EW["scalar"]})
                Y2 = P.step("vector", lambda e, c8=c8, h=h: e.scalar_tensor_tensor(
                    YG(c8)[:, h * 64:(h + 1) * 64], arB2[:, 63488 + h * 64:63488 + (h + 1) * 64],
                    stat[:, 320 + h:321 + h], YG(c8)[:, h * 64:(h + 1) * 64], OP.mult, OP.add),
                    waits={"sync": XSL["sync"], "vector": DPE["vector"]})
                ym[("yg2", c8, h)] = Y2["vector"]
        SCAN = P.snap()
        if stop <= 3:
            P.self_wait("gpsimd", P.cnt["gpsimd"]); P.self_wait("sync", P.cnt["sync"]); return nc

        gload = []
        for tt in range(8):
            GLD = P.step("sync", lambda e, tt=tt: e.dma_start(out=GATE(tt), in_=gate[tt * 128:(tt + 1) * 128, :]),
                         waits={"vector": ym[("yg2", 7, 31)]} if tt == 0 else {}, inc=16)
            gload.append(GLD["sync"])
        IDG0 = P.step("gpsimd", lambda e: (e.iota(scr[:, 0:128], pattern=[[1, 128]], base=0,
                                                  channel_multiplier=0, allow_small_or_imprecise_dtypes=True),
                                           e.iota(scr[:, 128:256], pattern=[[0, 128]], base=0,
                                                  channel_multiplier=1, allow_small_or_imprecise_dtypes=True))[-1],
                      waits={"tensor": LW["tensor"], "vector": ym[("yg2", 7, 31)]})
        IDV = P.step("vector", lambda e: (e.tensor_tensor(scr[:, 0:128], scr[:, 0:128], scr[:, 128:256],
                                                          OP.is_equal),
                                          e.tensor_copy(identb_t[:], scr[:, 0:128]))[-1],
                     waits={"gpsimd": IDG0["gpsimd"]})
        P.self_wait("vector", IDV["vector"])
        gm2 = {}
        for tt in range(8):
            SG = P.step("scalar", lambda e, tt=tt: e.activation(scr[:, 0:2048], GATE(tt), AF.Silu),
                        waits={"sync": gload[tt], "vector": max(IDV["vector"],
                               gm2.get(("gmul", tt - 1), ym[("yg2", 7, 31)]))})
            GM2 = P.step("vector", lambda e, tt=tt: e.tensor_tensor(YG(tt), YG(tt), scr[:, 0:2048], OP.mult),
                         waits={"scalar": SG["scalar"]})
            gm2[("gmul", tt)] = GM2["vector"]
            if stop <= 3.3:
                gm2[("grms", tt)] = GM2["vector"]
                continue
            SQ2 = P.step("scalar", lambda e, tt=tt: e.activation(scr[:, 0:2048], YG(tt), AF.Square),
                         waits={"vector": GM2["vector"]})
            RD = P.step("vector", lambda e, tt=tt: e.reduce_sum(stat[:, 448 + tt:449 + tt], scr[:, 0:2048], axis=AX.X),
                        waits={"scalar": SQ2["scalar"]})
            P.self_wait("vector", RD["vector"])
            RV = P.step("vector", lambda e, tt=tt: e.tensor_scalar(stat[:, 448 + tt:449 + tt],
                                                                   stat[:, 448 + tt:449 + tt], 1.0 / D, LN_EPS,
                                                                   OP.mult, OP.add))
            P.self_wait("vector", RV["vector"])
            RA0 = P.step("scalar", lambda e, tt=tt: e.activation(stat[:, 448 + tt:449 + tt],
                                                                 stat[:, 448 + tt:449 + tt], AF.Ln),
                         waits={"vector": RV["vector"]})
            P.self_wait("scalar", RA0["scalar"])
            RA = P.step("scalar", lambda e, tt=tt: e.activation(stat[:, 448 + tt:449 + tt],
                                                                stat[:, 448 + tt:449 + tt], AF.Exp, scale=-0.5))
            P.self_wait("scalar", RA["scalar"])
            GMR = P.step("vector", lambda e, tt=tt: e.tensor_scalar(YG(tt), YG(tt), stat[:, 448 + tt:449 + tt],
                                                                    None, OP.mult), waits={"scalar": RA["scalar"]})
            gm2[("grms", tt)] = GMR["vector"]
        tmarks = {}
        if stop <= 3.6:
            P.self_wait("gpsimd", P.cnt["gpsimd"]); P.self_wait("sync", P.cnt["sync"]); return nc
        for tt in range(8):
            for kt in range(16):
                ps = [psTB0, psTB1][kt % 2]
                wtr = {"vector": gm2[("grms", tt)]}
                if ("slot", kt % 2) in tmarks:
                    wtr["vector"] = max(wtr["vector"], tmarks[("slot", kt % 2)])
                TR = P.step("tensor", lambda e, tt=tt, kt=kt, ps=ps: e.transpose(
                    ps[:], YG(tt)[:, kt * 128:(kt + 1) * 128], identb_t[:]), waits=wtr)
                TE = P.step("vector", lambda e, tt=tt, kt=kt, ps=ps: e.tensor_copy(
                    GFM(kt)[:, tt * 128:(tt + 1) * 128], ps[:]), waits={"tensor": TR["tensor"]})
                tmarks[("gtr", tt, kt)] = TE["vector"]
                tmarks[("slot", kt % 2)] = TE["vector"]
        GT = P.snap()
        if stop <= 4:
            P.self_wait("gpsimd", P.cnt["gpsimd"]); P.self_wait("sync", P.cnt["sync"]); return nc

        NL = P.step("sync", lambda e: e.dma_start(out=stat[:, 464:480], in_=normw[:]), inc=16)
        def WO_S(i):
            return arB2[:, 53248 + (i % 2) * 4096:53248 + (i % 2) * 4096 + 4096]
        wo_marks = {}
        for cch in range(8):
            w = {"vector": tmarks[("gtr", 7, 15)]}
            if ("womm", cch - 2, 1) in wo_marks:
                w["tensor"] = wo_marks[("womm", cch - 2, 1)]
            GL2 = P.step("gpsimd", lambda e, cch=cch: e.dma_start(
                out=WO_S(cch).rearrange("p (k n) -> p k n", k=16),
                in_=woutT[:, cch * 256:(cch + 1) * 256].rearrange("(k p) n -> p k n", p=128)),
                waits=w, inc=16)
            NS = P.step("vector", lambda e, cch=cch: [
                e.tensor_scalar(WO_S(cch)[:, k * 256:(k + 1) * 256], WO_S(cch)[:, k * 256:(k + 1) * 256],
                                stat[:, 464 + k:465 + k], None, OP.mult) for k in range(16)][-1],
                waits={"gpsimd": GL2["gpsimd"], "sync": NL["sync"]})
            for ot in range(2):
                o = cch * 2 + ot
                slot = psM_s[o % 2]
                def wo_mm(e, cch=cch, ot=ot, slot=slot):
                    last = None
                    for tc in range(2):
                        for k in range(16):
                            last = e.matmul(slot[:, tc * 512:(tc + 1) * 512],
                                            WO_S(cch)[:, k * 256 + ot * 128:k * 256 + ot * 128 + 128],
                                            GFM(k)[:, tc * 512:(tc + 1) * 512],
                                            start=(k == 0), stop=(k == 15))
                    return last
                w2 = {"vector": NS["vector"]}
                if ("woev", o - 2) in wo_marks:
                    w2["vector"] = max(w2["vector"], wo_marks[("woev", o - 2)])
                MM = P.step("tensor", wo_mm, waits=w2)
                wo_marks[("womm", cch, ot)] = MM["tensor"]
                EV = P.step("vector", lambda e, o=o, slot=slot: e.tensor_tensor(Xt(o), Xt(o), slot, OP.add),
                            waits={"tensor": MM["tensor"]})
                wo_marks[("woev", o)] = EV["vector"]
                P.step("gpsimd", lambda e, o=o: e.dma_start(out=outT[o * 128:(o + 1) * 128, :], in_=Xt(o)),
                       waits={"vector": EV["vector"]}, inc=16)
        P.self_wait("gpsimd", P.cnt["gpsimd"])
        P.self_wait("sync", P.cnt["sync"])
    return nc


import ml_dtypes
from concourse.bass_utils import run_bass_kernel_spmd

_CACHE = {}

def _builders():
    if "A" not in _CACHE:
        _CACHE["A"] = build_A()
        _CACHE["B"] = build_B()
        _CACHE["C"] = build_C()
        _CACHE["D"] = build_D()
    return _CACHE["A"], _CACHE["B"], _CACHE["C"], _CACHE["D"]


def kernel(L0_attn_x, L0_attn_Wq_sem, L0_attn_Wk_sem, L0_attn_Wq_geo, L0_attn_Wk_geo, L0_attn_Wv, L0_attn_Wo,
           L0_ffn_gamma, L0_ffn_beta, L0_ffn_weight, L0_ffn_bias, L0_ffn_alpha,
           L1_attn_W_in, L1_attn_conv_w, L1_attn_conv_b, L1_attn_dt_bias, L1_attn_A_log, L1_attn_D_param,
           L1_attn_norm_w, L1_attn_W_out, L1_ffn_lut, L1_ffn_gate_idx, L1_ffn_up_idx, L1_ffn_down_idx):
    bf16 = ml_dtypes.bfloat16
    cores = list(range(8))
    x = np.ascontiguousarray(np.asarray(L0_attn_x, np.float32).reshape(8192, 2048))

    ncA, ncB, ncC, ncD = _builders()

    wq = qk_scrambled_weight(np.asarray(L0_attn_Wq_sem), np.asarray(L0_attn_Wq_geo))
    wk = qk_scrambled_weight(np.asarray(L0_attn_Wk_sem), np.asarray(L0_attn_Wk_geo))
    wqkvT = np.ascontiguousarray(np.concatenate([wq, wk, np.asarray(L0_attn_Wv)], axis=0).T.astype(np.float32))
    ffnwT = np.ascontiguousarray(np.asarray(L0_ffn_weight).T.astype(np.float32))
    ffnvec = np.ascontiguousarray(np.stack([
        np.asarray(L0_ffn_gamma), np.asarray(L0_ffn_beta),
        np.asarray(L0_ffn_bias), np.asarray(L0_ffn_alpha)], 0)
        .reshape(4, 16, 128).transpose(2, 0, 1).reshape(128, 64).astype(np.float32))
    hmask = headmask_host()
    in_A = []
    for c in cores:
        rows = slice(1024 * c, 1024 * (c + 1))
        pos = np.arange(1024 * (c % 4), 1024 * (c % 4 + 1))
        in_A.append({
            "xT": np.ascontiguousarray(x[rows].T), "wqkvT": wqkvT,
            "ropeC": rope_tile(pos, "cos"), "ropeS": rope_tile(pos, "sin"),
            "ffnwT": ffnwT, "ffnvec": ffnvec, "headmask": hmask,
        })
    rA = run_bass_kernel_spmd(ncA, in_A, core_ids=cores).results

    def vhostify(v_tm):
        T_ = v_tm.shape[0]
        return np.ascontiguousarray(v_tm.reshape(T_ // 128, 128, 16, 128).transpose(2, 1, 0, 3).reshape(2048, T_))
    woT = np.ascontiguousarray(np.asarray(L0_attn_Wo).T.astype(np.float32))
    winT = np.ascontiguousarray(np.asarray(L1_attn_W_in).T.astype(np.float32))
    kk_ = np.arange(128)
    trimask = np.zeros((128, 2048), np.float32)
    for j in range(4):
        trimask[:, 512 * j:512 * (j + 1)] = np.where(128 * j + kk_[:, None] <= np.arange(512)[None, :], 0.0, -1e30)
    trimask = trimask.astype(bf16)
    in_B = []
    for c in cores:
        b = c // 4
        kfull = np.ascontiguousarray(np.concatenate([np.asarray(rA[4 * b + i]["kT"]) for i in range(4)], axis=1))
        vfull_tm = np.concatenate([np.asarray(rA[4 * b + i]["vT"]) for i in range(4)], axis=1).T
        vown_tm = np.asarray(rA[c]["vT"]).T
        maskcol = np.zeros((128, 64), np.float32)
        for K in range(32):
            for g in range(2):
                kts = 128 * K + kk_
                maskcol[:, 2 * K + g] = np.where(kts < 1024 * (c % 4) + 512 * g, 0.0, -1e30)
        qk2 = np.zeros((16, 8), np.float32)
        qk2[:, 0] = rA[c]["qkn"][:, 0]
        for i in range(4):
            qk2[:, 1 + i] = rA[4 * b + i]["qkn"][:, 1]
        in_B.append({
            "accT": np.asarray(rA[c]["accT"]), "qTo": np.asarray(rA[c]["qT"]), "kTo": np.asarray(rA[c]["kT"]),
            "kfull": kfull, "vhost": vhostify(np.ascontiguousarray(vfull_tm)),
            "vown": vhostify(np.ascontiguousarray(vown_tm)),
            "maskcol": maskcol, "trimask": trimask, "qk2": qk2, "woT": woT, "winT": winT,
        })
    rB = run_bass_kernel_spmd(ncB, in_B, core_ids=cores).results

    conv_w = np.asarray(L1_attn_conv_w)
    convw = np.ascontiguousarray(conv_w.reshape(18, 128, 4).transpose(1, 0, 2).reshape(128, 72).astype(np.float32))
    convb = np.ascontiguousarray(np.asarray(L1_attn_conv_b).reshape(18, 128).T.astype(np.float32))
    dtvec = np.zeros((32, 4), np.float32)
    dtvec[:, 0] = np.asarray(L1_attn_dt_bias)
    dtvec[:, 1] = np.asarray(L1_attn_A_log)
    lutrow = np.ascontiguousarray(np.asarray(L1_ffn_lut).reshape(1, 16).astype(np.float32))
    PADV = 16
    gpad = np.full((6144, 2048), PADV, np.int32); gpad[:5504] = np.asarray(L1_ffn_gate_idx)
    upad = np.full((6144, 2048), PADV, np.int32); upad[:5504] = np.asarray(L1_ffn_up_idx)
    dpad = np.full((2048, 6144), PADV, np.int32); dpad[:, :5504] = np.asarray(L1_ffn_down_idx)
    strict = (kk_[:, None] > kk_[None, :]).astype(np.float32)
    triincl = (kk_[:, None] <= kk_[None, :]).astype(np.float32)
    triu01 = triincl.astype(bf16)
    in_C = []
    for c in cores:
        own = np.asarray(rB[c]["hbcdt"]).view(bf16) if np.asarray(rB[c]["hbcdt"]).dtype.kind == "V" else np.asarray(rB[c]["hbcdt"])
        if c % 4 == 0:
            bnd = np.zeros((3, 2336), bf16)
        else:
            prev = np.asarray(rB[c - 1]["hbcdt"])
            bnd = prev[-3:]
        hb = np.concatenate([bnd, own, np.zeros((125, 2336), bf16)], 0)
        in_C.append({
            "hbcdtP": np.ascontiguousarray(hb),
            "dtrawT": np.ascontiguousarray(own[:, 2304:2336].astype(np.float32).T),
            "convw": convw, "convb": convb, "dtvec": dtvec, "lutrow": lutrow,
            "gidxT": np.ascontiguousarray(gpad[768 * c:768 * (c + 1)].T),
            "uidxT": np.ascontiguousarray(upad[768 * c:768 * (c + 1)].T),
            "didxT": np.ascontiguousarray(dpad[:, 768 * c:768 * (c + 1)].T),
            "strict": strict, "triincl": triincl, "triu01": triu01,
        })
    rC = run_bass_kernel_spmd(ncC, in_C, core_ids=cores).results

    woutT = np.ascontiguousarray(np.asarray(L1_attn_W_out).T.astype(np.float32))
    normw = np.ascontiguousarray(np.asarray(L1_attn_norm_w).reshape(16, 128).T.astype(np.float32))
    dparam = np.ascontiguousarray(np.asarray(L1_attn_D_param).reshape(1, 32).astype(np.float32))
    gWall = np.ascontiguousarray(np.concatenate([np.asarray(rC[i]["gW"]) for i in range(8)], axis=1))
    uWall = np.ascontiguousarray(np.concatenate([np.asarray(rC[i]["uW"]) for i in range(8)], axis=1))
    dWall = np.ascontiguousarray(np.concatenate([np.asarray(rC[i]["dW"]) for i in range(8)], axis=0))
    in_D = []
    for c in cores:
        b, p = c // 4, c % 4
        hli4 = np.ascontiguousarray(np.concatenate(
            [np.asarray(rC[4 * b + i]["hloc"])[7 * 128:8 * 128] for i in range(4)], axis=0))
        logd4 = np.ascontiguousarray(np.stack(
            [np.asarray(rC[4 * b + i]["segld"])[0] for i in range(4)], axis=0).astype(np.float32))
        m2 = np.zeros((4, 4), np.float32)
        for j in range(4):
            for r2 in range(4):
                if j < r2 < p:
                    m2[j, r2] = 1.0
        in_D.append({
            "x1T": np.asarray(rB[c]["x1T"]), "xn1bT": np.asarray(rB[c]["xn1bT"]),
            "gate": np.asarray(rB[c]["gate"]), "yintra": np.asarray(rC[c]["yintra"]),
            "xstok": np.asarray(rC[c]["xstok"]), "cfm": np.asarray(rC[c]["cfm_o"]),
            "wtok": np.ascontiguousarray(np.asarray(rC[c]["wtok_o"]).astype(np.float32)),
            "hlocin": np.asarray(rC[c]["hloc"]), "hli4": hli4, "logd4": logd4,
            "mask2T": np.ascontiguousarray(np.concatenate([m2.T, np.zeros((4, 124), np.float32)], 1)),
            "mseg": (np.arange(4) < p).astype(np.float32).reshape(4, 1),
            "dparam": dparam, "normw": normw, "woutT": woutT,
            "gWall": gWall, "uWall": uWall, "dWall": dWall,
        })
    rD = run_bass_kernel_spmd(ncD, in_D, core_ids=cores).results

    out = np.concatenate([np.asarray(rD[c]["outT"]).T for c in cores], axis=0)
    return np.ascontiguousarray(out.reshape(2, 4096, 2048))
